# version 23
# baseline (speedup 1.0000x reference)
import os
import numpy as np
import ml_dtypes
import concourse.bass as bass
import concourse.mybir as mybir
from concourse.bass_utils import run_bass_kernel_spmd
from concourse import library_config

F32 = mybir.dt.float32
BF16 = mybir.dt.bfloat16
I32 = mybir.dt.int32
AF = mybir.ActivationFunctionType
ALU = mybir.AluOpType
AX = mybir.AxisListType

D = 2048
S_FULL = 4096
TT = 512
NT_FULL = S_FULL // TT
NKC = D // 128
NOC = 32
NIC = 43
HV = 16
HQ = 8
DKSC = float(128.0 ** -0.5)


class _Buf:
    __slots__ = ("last_w", "readers")

    def __init__(self):
        self.last_w = None
        self.readers = []


ENGS = ("PE", "V", "SC", "G", "SY")


class Prog:

    N_SY_LANES = 12
    N_G_LANES = 2

    def __init__(self):
        self.ops = {e: [] for e in ENGS}
        self.bufs = {}
        self.barrier_at = None
        self.lane_count = {"SY": 0, "G": 0}

    def buf(self, *key):
        b = self.bufs.get(key)
        if b is None:
            b = _Buf()
            self.bufs[key] = b
        return b

    def op(self, eng, emit, reads=(), writes=(), reads_nodep=(), dma=False,
           cc=False, tag=""):
        real = "G" if eng == "GD" else eng
        dma = dma or (eng == "GD")
        lane = None
        if cc:
            i = self.lane_count.setdefault("CC", 0)
            self.lane_count["CC"] = i + 1
            lane = ("CC", i, 0)
        elif dma:
            lane = (real, -1, -1)
        idx = len(self.ops[real])
        deps = set()
        soft = set()

        def add(e, i):
            if e == "PE" and real == "PE" and self.ops["PE"][i][3] is None:
                soft.add(i)
                return
            deps.add((e, i))
        for b in reads:
            if b.last_w is not None:
                add(*b.last_w)
        for b in writes:
            if b.last_w is not None:
                add(*b.last_w)
            for r in b.readers:
                add(*r)
        self.ops[real].append([emit, sorted(deps),
                               False, lane, tag, sorted(soft)])
        me = (real, idx)
        for b in reads:
            b.readers.append(me)
        for b in reads_nodep:
            b.readers.append(me)
        for b in writes:
            b.last_w = me
            b.readers = []

    def barrier(self):
        tails = []
        seen_lane = set()
        for e in ENGS:
            got_compute = False
            for i in range(len(self.ops[e]) - 1, -1, -1):
                o = self.ops[e][i]
                if o[3] is None:
                    if not got_compute:
                        tails.append((e, i))
                        got_compute = True
                else:
                    key = (o[3][0], o[3][1])
                    if key not in seen_lane:
                        seen_lane.add(key)
                        tails.append((e, i))

        self.barrier_at = ({e: len(self.ops[e]) for e in ENGS}, tails)

    EST = {
        "PE": {"qkmm": 310, "gmm": 310, "omm": 310, "fgmm": 310, "fumm": 310,
               "fdmm": 310, "bamm": 310, "ks0": 135, "qs0": 135, "akk": 135,
               "akq": 135, "ssq1": 135, "ssq2": 135, "sapi": 135, "sapg": 135,
               "oin": 135, "snmm": 135, "lntr": 150, "vtr": 150, "ktr": 150,
               "g1tr": 150, "ctr": 150, "otr": 150, "p0tr": 150, "dqtr": 150,
               "trL": 150, "ltr": 200, "rcm": 150, "l2mm": 300, "krep": 350,
               "p0ba": 150, "_d": 200},
        "V": {"lnred": 2300, "lnn": 1250, "cvev": 700, "cvtl": 80, "cvnt": 80,
              "cv3": 260, "cvk": 420, "l2sq": 420, "khat": 800, "sapev": 460,
              "gb2": 460, "mb2": 320, "rt": 460, "g1ev": 400, "osum": 460,
              "orsq": 360, "orred": 360, "gmul": 460, "sup": 460, "sbf": 320,
              "vtev": 620, "rkcv": 90, "rqcv": 90, "grq": 70, "dts": 320,
              "dqj": 1600, "dqa": 3200, "dqcv": 2600, "dq0": 3000,
              "lnev": 1300, "oadd": 1300, "oev": 700, "fdev": 1500,
              "fmul": 700, "uev": 700, "trLev": 110, "dbg": 1300, "dbm": 1300,
              "scanL": 400, "scanS": 400, "bala": 600, "_d": 450},
        "SC": {"akkev": 380, "akqev": 380, "ssqev": 540, "ub": 470,
               "kdec": 470, "onb": 470, "oea": 470, "orln": 340, "orex": 340, "orsq": 540,
               "dexp": 1700, "basig": 940, "baexp": 940, "gsl": 760,
               "cvsl": 760, "fsl": 760, "l2ln": 940, "l2ex": 940,
               "lnsq": 3400, "p0ev": 1700, "ktev": 380, "_d": 540},
        "G": {"lbc": 900, "_d": 2500},
        "SY": {"qkld": 2600, "gld": 2600, "old": 2600, "fgld": 2600,
               "fuld": 2600, "fdwl": 2200, "fdhl": 1700, "lnld": 3600,
               "xold": 3600, "ost": 3600, "dqld": 6000, "p0st": 2600,
               "fhst": 1700, "_d": 3000},
        "GD": {"p0ld": 4200, "_d": 4200},
        "CC": {"_d": 160000},
    }

    def _est(self, pool, tag):
        t = self.EST.get(pool, self.EST["SY"])
        return t.get(tag, t["_d"])

    def finalize(self, nc, sems, lane_sems):
        import heapq
        bar_marks_orig = (self.barrier_at[0] if self.barrier_at is not None
                          else {e: 0 for e in ENGS})
        nodes = []
        id_of = {}
        for e in ENGS:
            for i in range(len(self.ops[e])):
                id_of[(e, i)] = len(nodes)
                nodes.append((e, i))
        N = len(nodes)
        succ = [[] for _ in range(N)]
        indeg = [0] * N
        hard = [None] * N
        for e in ENGS:
            for i, o in enumerate(self.ops[e]):
                me = id_of[(e, i)]
                for (de, di) in o[1]:
                    succ[id_of[(de, di)]].append((me, True))
                    indeg[me] += 1
                for si in o[5]:
                    succ[id_of[("PE", si)]].append((me, False))
                    indeg[me] += 1
        pool_of = [None] * N
        epoch = [0] * N
        durs = [0] * N
        for e in ENGS:
            for i, o in enumerate(self.ops[e]):
                nid = id_of[(e, i)]
                if o[3] is None:
                    p = e
                elif o[3][0] == "CC":
                    p = "CC"
                elif e == "G":
                    p = "GD"
                else:
                    p = "SY"
                pool_of[nid] = p
                tag = o[4] if o[4] else "_d"
                durs[nid] = self._est(p if p in self.EST else e, tag)
                epoch[nid] = 0 if i < bar_marks_orig.get(e, 0) else 1
        if self.barrier_at is None:
            epoch = [0] * N
        cp = [0.0] * N
        order_topo = []
        indeg2 = indeg[:]
        stack = [i for i in range(N) if not succ[i]]
        nsucc = [len(succ[i]) for i in range(N)]
        rem = nsucc[:]
        stack = [i for i in range(N) if rem[i] == 0]
        preds = [[] for _ in range(N)]
        for i in range(N):
            for (m, _h) in succ[i]:
                preds[m].append(i)
        while stack:
            i = stack.pop()
            cp[i] += durs[i]
            for p2 in preds[i]:
                if cp[i] > cp[p2]:
                    cp[p2] = cp[i]
                rem[p2] -= 1
                if rem[p2] == 0:
                    stack.append(p2)

        HOP = 150
        n_units = {"PE": 1, "V": 1, "SC": 1, "G": 1, "SY": self.N_SY_LANES,
                   "GD": self.N_G_LANES, "CC": 3}
        start_t = [0.0] * N
        end_t = [0.0] * N
        lane_pick = [None] * N
        sched_order = []
        for ep in (0, 1):
            free = {p: [0.0] * n for p, n in n_units.items()}
            if ep == 1 and self.barrier_at is not None:
                tbar = max([0.0] + [end_t[i] for i in range(N) if epoch[i] == 0])
                free = {p: [tbar] * n for p, n in n_units.items()}
            ready = {p: [] for p in n_units}
            rt = [0.0] * N
            cnt = [0] * N
            for i in range(N):
                if epoch[i] != ep:
                    continue
                need = 0
                for e2 in ENGS:
                    pass
                cnt[i] = indeg[i]
            for e in ENGS:
                for i, o in enumerate(self.ops[e]):
                    nid = id_of[(e, i)]
                    if epoch[nid] != ep:
                        continue
                    c = 0
                    r = 0.0
                    for (de, di) in o[1]:
                        did = id_of[(de, di)]
                        if epoch[did] == ep:
                            c += 1
                        else:
                            r = max(r, end_t[did] + HOP)
                    for si in o[5]:
                        did = id_of[("PE", si)]
                        if epoch[did] == ep:
                            c += 1
                        else:
                            r = max(r, end_t[did])
                    cnt[nid] = c
                    rt[nid] = r
                    if c == 0:
                        heapq.heappush(ready[pool_of[nid]], (rt[nid], nid))
            n_done = 0
            n_total = sum(1 for i in range(N) if epoch[i] == ep)
            while n_done < n_total:
                best = None
                for p, h in ready.items():
                    if not h:
                        continue
                    r0, nid0 = h[0]
                    fmin = min(free[p])
                    stt = max(r0, fmin)
                    if best is None or stt < best[0]:
                        best = (stt, p, nid0)
                assert best is not None, "scheduler deadlock"
                stt, p, nid = best
                pool_h = ready[p]
                cand = []
                while pool_h and pool_h[0][0] <= stt + 240:
                    cand.append(heapq.heappop(pool_h))
                sel = max(range(len(cand)), key=lambda k: cp[cand[k][1]])
                nid = cand[sel][1]
                for k in range(len(cand)):
                    if k != sel:
                        heapq.heappush(pool_h, cand[k])
                li = min(range(n_units[p]), key=lambda k: free[p][k])
                stt = max(rt[nid], free[p][li])
                start_t[nid] = stt
                end_t[nid] = stt + durs[nid]
                free[p][li] = end_t[nid]
                lane_pick[nid] = li
                sched_order.append(nid)
                n_done += 1
                for (m, is_hard) in succ[nid]:
                    if epoch[m] != ep:
                        continue
                    rt[m] = max(rt[m], end_t[nid] + (HOP if is_hard else 0))
                    cnt[m] -= 1
                    if cnt[m] == 0:
                        heapq.heappush(ready[pool_of[m]], (rt[m], m))
        lane_seq = {}
        for nid in sorted(range(N), key=lambda i: (start_t[i], i)):
            e, i = nodes[nid]
            o = self.ops[e][i]
            if o[3] is None:
                continue
            if o[3][0] == "CC":
                continue
            q = "SY" if pool_of[nid] == "SY" else "G"
            key = (q, lane_pick[nid])
            s = lane_seq.get(key, 0)
            lane_seq[key] = s + 1
            o[3] = (q, lane_pick[nid], s)
        new_order = {e: [] for e in ENGS}
        for nid in sorted(range(N), key=lambda i: (start_t[i], i)):
            e, i = nodes[nid]
            new_order[e].append(i)
        for e in ENGS:
            for o in self.ops[e]:
                for (de, di) in o[1]:
                    self.ops[de][di][2] = True
        val = {}
        for e in ENGS:
            cc = 0
            for i in new_order[e]:
                o = self.ops[e][i]
                if o[3] is not None:
                    o[2] = True
                    step = 1 if o[3][0] == "CC" else 16
                    val[(e, i)] = ("L", o[3][0], o[3][1], step * (o[3][2] + 1))
                else:
                    if o[2]:
                        cc += 1
                    val[(e, i)] = ("C", e, cc)
        eng_map = {"PE": nc.tensor, "V": nc.vector, "SC": nc.scalar,
                   "G": nc.gpsimd, "SY": nc.sync}

        def sem_of(v):
            if v[0] == "C":
                return sems[v[1]], v[2], ("C", v[1])
            return lane_sems[(v[1], v[2])], v[3], ("L", v[1], v[2])

        bar_tails = self.barrier_at[1] if self.barrier_at is not None else []
        for e in ENGS:
            engine = eng_map[e]
            sat = {}
            bar_fired = self.barrier_at is None
            for i in new_order[e]:
                o = self.ops[e][i]
                deps = list(o[1])
                if not bar_fired and epoch[id_of[(e, i)]] == 1:
                    bar_fired = True
                    for (te, tidx) in bar_tails:
                        to = self.ops[te][tidx]
                        if to[3] is not None or te != e:
                            deps.append((te, tidx))
                want = {}
                for (de, di) in deps:
                    sem, tv, key = sem_of(val[(de, di)])
                    if tv > want.get(key, (0, None))[0]:
                        want[key] = (tv, sem)
                for key, (tv, sem) in want.items():
                    if tv <= sat.get(key, 0):
                        continue
                    engine.wait_ge(sem, tv)
                    sat[key] = tv
                if o[3] is not None and o[3][0] != "CC" and o[3][2] > 0:
                    key = ("L", o[3][0], o[3][1])
                    tv = 16 * o[3][2]
                    if tv > sat.get(key, 0):
                        engine.wait_ge(lane_sems[(o[3][0], o[3][1])], tv)
                        sat[key] = tv
                inst = o[0]()
                if o[2] and inst is not None:
                    if o[3] is not None:
                        inst.then_inc(lane_sems[(o[3][0], o[3][1])],
                                      1 if o[3][0] == "CC" else 16)
                    else:
                        inst.then_inc(sems[e], 1)
        engine = eng_map["SY"]
        lane_tot = {}
        for e in ENGS:
            ncnt = 0
            for o in self.ops[e]:
                if o[3] is not None:
                    key = (o[3][0], o[3][1])
                    step = 1 if o[3][0] == "CC" else 16
                    lane_tot[key] = max(lane_tot.get(key, 0),
                                        step * (o[3][2] + 1))
                elif o[2]:
                    ncnt += 1
            if ncnt:
                engine.wait_ge(sems[e], ncnt)
        for key, tv in lane_tot.items():
            engine.wait_ge(lane_sems[key], tv)
        self.sched_makespan = max(end_t) if N else 0.0
        if os.environ.get("KDIAG"):
            import collections
            busy = collections.Counter()
            tagsum = collections.Counter()
            for nid in range(N):
                busy[pool_of[nid]] += durs[nid]
                e, i = nodes[nid]
                tagsum[(pool_of[nid], self.ops[e][i][4])] += durs[nid]
            ms = self.sched_makespan
            print(f"[sched] makespan {ms/1e6:.2f}ms")
            for p, b in busy.most_common():
                print(f"[sched] {p:3s} busy {b/1e6:7.2f}ms  util {b/ms*100:4.0f}%")
            for (p, t), b in tagsum.most_common(24):
                print(f"[sched]   {p:3s} {t:6s} {b/1e6:7.2f}ms")
            pred_best = [None] * N
            for nid in range(N):
                e, i = nodes[nid]
                o = self.ops[e][i]
                for (de, di) in o[1]:
                    pred_best[nid] = pred_best[nid]
            last = max(range(N), key=lambda i: end_t[i])
            path = []
            cur = last
            for _ in range(100000):
                e, i = nodes[cur]
                o = self.ops[e][i]
                cands = [id_of[d] for d in o[1]] + [id_of[("PE", s)] for s in o[5]]
                path.append(cur)
                best = None
                for d in cands:
                    if best is None or end_t[d] > end_t[best]:
                        best = d
                if best is None or end_t[best] + 1500 < start_t[cur]:
                    break
                cur = best
            cp = collections.Counter()
            for nid in path:
                e, i = nodes[nid]
                cp[(pool_of[nid], self.ops[e][i][4])] += durs[nid]
            print(f"[sched] cp-walk len {len(path)} dur {sum(durs[n] for n in path)/1e6:.2f}ms (gaps excl)")
            for (p, t), b in cp.most_common(14):
                print(f"[sched]   CP {p:3s} {t:6s} {b/1e6:6.2f}ms")


def build_program(n_tiles=NT_FULL, lut_vals=None):
    if lut_vals is None:
        lut_vals = [0.0] * 16
    from contextlib import ExitStack
    nc = bass.Bass("TRN2", target_bir_lowering=False, debug=False)

    x_d = nc.dram_tensor("x", [S_FULL, D], F32, kind="ExternalInput").ap()
    wqkv_d = nc.dram_tensor("w_qkv", [4096, D], F32, kind="ExternalInput").ap()
    wg_d = nc.dram_tensor("w_gate", [D, D], F32, kind="ExternalInput").ap()
    wo_d = nc.dram_tensor("w_out", [D, D], F32, kind="ExternalInput").ap()
    wba_d = nc.dram_tensor("w_ba", [32, D], F32, kind="ExternalInput").ap()
    convw_d = nc.dram_tensor("conv_w", [4096, 4], F32, kind="ExternalInput").ap()
    logA_d = nc.dram_tensor("log_A", [16, 1], F32, kind="ExternalInput").ap()
    dtb_d = nc.dram_tensor("dt_bias", [16, 1], F32, kind="ExternalInput").ap()
    rmsw_d = nc.dram_tensor("rms_w", [128, 1], F32, kind="ExternalInput").ap()
    myg_d = nc.dram_tensor("my_g", [6 * 128, D], I32, kind="ExternalInput").ap()
    myu_d = nc.dram_tensor("my_u", [6 * 128, D], I32, kind="ExternalInput").ap()
    myd_d = nc.dram_tensor("my_d", [2 * 128, 5504], I32, kind="ExternalInput").ap()
    cidb_d = nc.dram_tensor("cid_bf", [128, 128], BF16, kind="ExternalInput").ap()
    cidf_d = nc.dram_tensor("cid_f", [128, 128], F32, kind="ExternalInput").ap()
    cincl_d = nc.dram_tensor("cm_incl", [128, 128], BF16, kind="ExternalInput").ap()
    csneg_d = nc.dram_tensor("cm_sneg", [128, 128], BF16, kind="ExternalInput").ap()
    cincl8_d = nc.dram_tensor("cm_incl8", [128, 1024], BF16, kind="ExternalInput").ap()
    csneg8_d = nc.dram_tensor("cm_sneg8", [128, 1024], BF16, kind="ExternalInput").ap()
    out_d = nc.dram_tensor("out", [S_FULL, D], F32, kind="ExternalOutput").ap()

    s_wq = nc.dram_tensor("s_wq", [128, NOC, NKC, 128], BF16, kind="Internal").ap()
    s_wg = nc.dram_tensor("s_wg", [128, NKC, NKC, 128], BF16, kind="Internal").ap()
    s_wo = nc.dram_tensor("s_wo", [128, NKC, NKC, 128], BF16, kind="Internal").ap()
    ag_g_in = nc.dram_tensor("ag_g_in", [6, 128, NKC * 128], BF16, kind="Internal").ap()
    ag_u_in = nc.dram_tensor("ag_u_in", [6, 128, NKC * 128], BF16, kind="Internal").ap()
    ag_d_in = nc.dram_tensor("ag_d_in", [2, 128, NIC * 128], BF16, kind="Internal").ap()
    ag_g = nc.dram_tensor("ag_g", [48, 128, NKC * 128], BF16, kind="Internal").ap()
    ag_u = nc.dram_tensor("ag_u", [48, 128, NKC * 128], BF16, kind="Internal").ap()
    ag_d = nc.dram_tensor("ag_d", [16, 128, NIC * 128], BF16, kind="Internal").ap()
    s_hact = nc.dram_tensor("s_hact", [NIC, 128, TT], BF16, kind="Internal").ap()

    P = Prog()
    bm = P.buf
    st = ExitStack()
    sb = lambda name, shape, dt: st.enter_context(nc.sbuf_tensor(name, shape, dt))
    ps = lambda name, shape, dt: st.enter_context(nc.psum_tensor(name, shape, dt))

    sems = {e: st.enter_context(nc.semaphore(f"sem_{e}")) for e in ENGS}
    lane_sems = {}
    for i in range(Prog.N_SY_LANES):
        lane_sems[("SY", i)] = st.enter_context(nc.semaphore(f"dsy{i}"))
    for i in range(Prog.N_G_LANES):
        lane_sems[("G", i)] = st.enter_context(nc.semaphore(f"dg{i}"))
    for i in range(3):
        lane_sems[("CC", i)] = st.enter_context(nc.semaphore(f"cc{i}"))

    m_idb = sb("m_idb", [128, 128], BF16)
    m_idf = sb("m_idf", [128, 128], F32)
    m_incl = sb("m_incl", [128, 128], BF16)
    m_sneg = sb("m_sneg", [128, 128], BF16)
    m_incl8 = sb("m_incl8", [128, 1024], BF16)
    m_sneg8 = sb("m_sneg8", [128, 1024], BF16)
    ones_rb = sb("ones_rb", [1, 128], BF16)
    ones_cb = sb("ones_cb", [128, 1], BF16)
    one_b = sb("one_b", [1, 1], BF16)
    wbaT = sb("wbaT", [128, NKC, 64], BF16)
    convw = sb("convw", [128, NOC, 4], F32)
    rmsw = sb("rmsw", [128, 1], F32)
    logA_c = sb("logA_c", [16, 1], F32)
    negA = sb("negA", [16, 1], F32)
    dtb = sb("dtb", [16, 1], F32)
    eps5 = sb("eps5", [128, 1], F32)
    eps6 = sb("eps6", [128, 1], F32)

    B = [ps(f"pb{i}", [128, 512], F32) for i in range(6)]
    BT = [ps("pbt0", [128, 1024], BF16), ps("pbt1", [128, 1024], BF16)]

    P.op("G", lambda: nc.gpsimd.load_library(library_config.proxy))
    P.op("SY", lambda: nc.sync.dma_start(m_idb[:], cidb_d[:]),
         writes=[bm("m_idb")], dma=True)
    P.op("SY", lambda: nc.sync.dma_start(m_idf[:], cidf_d[:]),
         writes=[bm("m_idf")], dma=True)
    P.op("SY", lambda: nc.sync.dma_start(m_incl[:], cincl_d[:]),
         writes=[bm("m_incl")], dma=True)
    P.op("SY", lambda: nc.sync.dma_start(m_sneg[:], csneg_d[:]),
         writes=[bm("m_sneg")], dma=True)
    P.op("SY", lambda: nc.sync.dma_start(m_incl8[:], cincl8_d[:]),
         writes=[bm("m_incl8")], dma=True)
    P.op("SY", lambda: nc.sync.dma_start(m_sneg8[:], csneg8_d[:]),
         writes=[bm("m_sneg8")], dma=True)
    P.op("SY", lambda: nc.sync.dma_start(
        convw[:, :, :], convw_d.rearrange("(a p) k -> p a k", p=128)),
         writes=[bm("convw")], dma=True)
    P.op("SY", lambda: nc.sync.dma_start(rmsw[:], rmsw_d[:]),
         writes=[bm("rmsw")], dma=True)
    P.op("SY", lambda: nc.sync.dma_start(logA_c[:], logA_d[:]),
         writes=[bm("logA")], dma=True)
    P.op("SY", lambda: nc.sync.dma_start(dtb[:], dtb_d[:]),
         writes=[bm("dtb")], dma=True)
    P.op("V", lambda: nc.vector.memset(ones_rb[:1, :], 1.0), writes=[bm("ones_rb")])
    P.op("V", lambda: nc.vector.memset(ones_cb[:], 1.0), writes=[bm("ones_cb")])
    P.op("V", lambda: nc.vector.memset(one_b[:1, :], 1.0), writes=[bm("one_b")])
    P.op("V", lambda: nc.vector.memset(eps5[:], 1e-5), writes=[bm("eps5")])
    P.op("V", lambda: nc.vector.memset(eps6[:], 1e-6), writes=[bm("eps6")])
    P.op("SC", lambda: nc.scalar.activation(negA[:], logA_c[:], AF.Exp),
         reads=[bm("logA")], writes=[bm("negA")])
    P.op("V", lambda: nc.vector.tensor_scalar(negA[:], negA[:], -1.0, None, ALU.mult),
         reads=[bm("negA")], writes=[bm("negA")])


    st0 = ExitStack()
    sb0 = lambda name, shape, dt: st0.enter_context(nc.sbuf_tensor(name, shape, dt))
    wcast = [sb0(f"p0_wcast{i}", [128, D], BF16) for i in range(2)]
    q_idx = sb0("p0_idx", [128, 5504], I32)
    q_idxb = sb0("p0_idxb", [128, 5504], BF16)
    q_w = sb0("p0_w", [128, 5504], BF16)
    q_t = sb0("p0_t", [128, 5504], BF16)
    q_g = sb0("p0_g", [128, 5504], BF16)
    dstage = [sb0(f"p0_dst{i}", [128, NIC * 128], BF16) for i in range(2)]
    wba_f = sb0("p0_wba", [32, D], F32)
    wba_b = sb0("p0_wbab", [32, D], BF16)

    def transpose_store(src_sb, src_buf, stage_sb, stage_buf, dst_ap, dst_buf, nkc):
        for half in range((nkc + 7) // 8):
            k0 = half * 8
            k1 = min(k0 + 8, nkc)
            bi = half % 2
            for kc in range(k0, k1):
                P.op("PE", (lambda kc=kc, k0=k0, bi=bi, s=src_sb:
                            nc.tensor.transpose(
                                BT[bi][:, (kc - k0) * 128:(kc - k0 + 1) * 128],
                                s[:, kc * 128:(kc + 1) * 128], m_idb[:])),
                     reads=[src_buf, bm("m_idb")], writes=[bm("BT", bi)], tag="p0tr")
            P.op("SC", (lambda k0=k0, k1=k1, bi=bi, s=stage_sb:
                        nc.scalar.copy(s[:, k0 * 128:k1 * 128],
                                       BT[bi][:, 0:(k1 - k0) * 128])),
                 reads=[bm("BT", bi)], writes=[stage_buf], tag="p0ev")
        P.op("SY", (lambda s=stage_sb, d=dst_ap, nk=nkc:
                    nc.sync.dma_start(d, s[:, 0:nk * 128].rearrange(
                        "p (a b) -> p a b", b=128))),
             reads=[stage_buf], writes=[dst_buf], dma=True, tag="p0st")

    for (w_ap, scr, n_oc, sn) in ((wqkv_d, s_wq, NOC, "s_wq"),
                                  (wg_d, s_wg, NKC, "s_wg"),
                                  (wo_d, s_wo, NKC, "s_wo")):
        for oc in range(n_oc):
            sl = oc % 2
            P.op("GD", (lambda w=w_ap, oc=oc, t=wcast[sl]:
                        nc.gpsimd.dma_start(t[:], w[oc * 128:(oc + 1) * 128, :])),
                 writes=[bm("wcast", sl)], tag="p0ld")
            transpose_store(wcast[sl], bm("wcast", sl), dstage[sl],
                            bm("dstage", sl), scr[:, oc, :, :], bm(sn, oc), NKC)

    P.op("V", lambda: nc.vector.memset(wbaT[:].rearrange("p a b -> p (a b)"), 0.0),
         writes=[bm("wbaT")])
    P.op("SY", lambda: nc.sync.dma_start(wba_f[:], wba_d[:]),
         writes=[bm("wba_f")], dma=True)
    P.op("V", lambda: nc.vector.tensor_copy(wba_b[:], wba_f[:]),
         reads=[bm("wba_f")], writes=[bm("wba_b")])
    for kc in range(NKC):
        P.op("PE", (lambda kc=kc: nc.tensor.transpose(
            BT[0][:, 0:32], wba_b[0:32, kc * 128:(kc + 1) * 128],
            m_idb[0:32, 0:32])),
             reads=[bm("wba_b"), bm("m_idb")], writes=[bm("BT", 0)])
        P.op("V", (lambda kc=kc: nc.vector.tensor_copy(wbaT[:, kc, 0:16],
                                                       BT[0][:, 0:16])),
             reads=[bm("BT", 0)], writes=[bm("wbaT")])
        P.op("V", (lambda kc=kc: nc.vector.tensor_copy(wbaT[:, kc, 32:48],
                                                       BT[0][:, 16:32])),
             reads=[bm("BT", 0)], writes=[bm("wbaT")])

    for (idx_ap, agin, n_rb, n_cols, inname) in (
            (myg_d, ag_g_in, 6, D, "ag_g_in"),
            (myu_d, ag_u_in, 6, D, "ag_u_in"),
            (myd_d, ag_d_in, 2, 5504, "ag_d_in")):
        nkc = n_cols // 128
        for rb in range(n_rb):
            P.op("SY", (lambda a=idx_ap, rb=rb, n=n_cols:
                        nc.sync.dma_start(q_idx[:, 0:n], a[rb * 128:(rb + 1) * 128, :])),
                 writes=[bm("q_idx")], dma=True, tag="dqld")
            P.op("V", (lambda n=n_cols: nc.vector.tensor_copy(q_idxb[:, 0:n],
                                                              q_idx[:, 0:n])),
                 reads=[bm("q_idx")], writes=[bm("q_idxb")], tag="dqcv")
            P.op("V", (lambda n=n_cols: nc.vector.tensor_scalar(
                q_w[:, 0:n], q_idxb[:, 0:n], 0.0, float(lut_vals[0]),
                ALU.is_equal, ALU.mult)),
                 reads=[bm("q_idxb")], writes=[bm("q_w")], tag="dq0")
            for j in range(1, 16):
                P.op("V", (lambda n=n_cols, j=j: nc.vector.tensor_single_scalar(
                    q_t[:, 0:n], q_idxb[:, 0:n], float(j), ALU.is_equal)),
                     reads=[bm("q_idxb")], writes=[bm("q_t")], tag="dqj")
                a_src, a_dst = (q_w, q_g) if j % 2 == 1 else (q_g, q_w)
                b_src = bm("q_w") if j % 2 == 1 else bm("q_g")
                b_dst = bm("q_g") if j % 2 == 1 else bm("q_w")
                P.op("V", (lambda n=n_cols, j=j, a=a_src, d=a_dst:
                           nc.vector.scalar_tensor_tensor(
                    d[:, 0:n], q_t[:, 0:n], float(lut_vals[j]), a[:, 0:n],
                    ALU.mult, ALU.add)),
                     reads=[bm("q_t"), b_src], writes=[b_dst], tag="dqa")
            transpose_store(q_g, bm("q_g"), dstage[rb % 2], bm("dstage", rb % 2),
                            agin[rb, :, :].rearrange("p (a b) -> p a b", b=128),
                            bm(inname, rb), nkc)
    P.op("G", lambda: nc.gpsimd.collective_compute(
        "AllGather", ALU.bypass, replica_groups=[list(range(8))],
        ins=[ag_g_in[:, :, :].opt()], outs=[ag_g[:, :, :].opt()]),
         reads=[bm("ag_g_in", b) for b in range(6)],
         writes=[bm("s_fg", oc) for oc in range(48)] + [bm("cc_ord")],
         cc=True, tag="agg")
    P.op("G", lambda: nc.gpsimd.collective_compute(
        "AllGather", ALU.bypass, replica_groups=[list(range(8))],
        ins=[ag_u_in[:, :, :].opt()], outs=[ag_u[:, :, :].opt()]),
         reads=[bm("ag_u_in", b) for b in range(6)],
         writes=[bm("s_fu", oc) for oc in range(48)] + [bm("cc_ord")],
         cc=True, tag="agu")
    P.op("G", lambda: nc.gpsimd.collective_compute(
        "AllGather", ALU.bypass, replica_groups=[list(range(8))],
        ins=[ag_d_in[:, :, :].opt()], outs=[ag_d[:, :, :].opt()]),
         reads=[bm("ag_d_in", b) for b in range(2)],
         writes=[bm("s_fd", hc) for hc in range(16)] + [bm("cc_ord")],
         cc=True, tag="agd")

    st0.close()
    P.barrier()

    xnT = sb("xnT", [128, NKC, TT], BF16)
    cooked = sb("cooked", [128, NOC, TT], BF16)
    gateb = sb("gateb", [128, NKC, TT], BF16)
    accb = sb("accb", [128, NKC, TT], BF16)
    hstage = [sb(f"hstage{i}", [128, TT], BF16) for i in range(2)]
    ub_t = [sb(f"ubt{i}", [128, TT], BF16) for i in range(2)]
    hring = [sb(f"hring{i}", [128, TT], BF16) for i in range(4)]
    S32 = sb("S32", [128, HV, 128], F32)
    Sbf = sb("Sbf", [128, HV, 128], BF16)
    tails = sb("tails", [128, NOC, 3], F32)
    raw_ext = sb("raw_ext", [128, 515], BF16)
    cvt_b = sb("cvt_b", [128, 512], BF16)
    sg_b = sb("sg_b", [128, 512], BF16)
    xst = sb("xst", [128, D], F32)
    xst2 = sb("xst2", [128, D], F32)
    xout = sb("xout", [128, D], F32)
    xntok = sb("xntok", [128, D], BF16)
    mcol = sb("mcol", [128, 1], F32)
    s2col = sb("s2col", [128, 1], F32)
    vtmp1 = sb("vtmp1", [128, 1], F32)
    rstd = sb("rstd", [128, 1], F32)
    rrow_f = sb("rrow_f", [1, TT], F32)
    rrow_b = sb("rrow_b", [1, TT], BF16)
    rkcols = sb("rkcols", [128, HQ, 4], F32)
    rqcols = sb("rqcols", [128, HQ, 4], F32)
    la_f = sb("la_f", [16, TT], F32)
    suf_f = sb("suf_f", [16, TT], F32)
    beta_f = sb("beta_f", [16, TT], F32)
    sp_t = sb("sp_t", [16, TT], F32)
    ones16 = sb("ones16", [16, 128], F32)
    Lt = sb("Lt", [128, 4, 16], F32)
    sufT = sb("sufT", [128, 4, 16], F32)
    laT = sb("laT", [128, 4, 16], F32)
    betaT = sb("betaT", [128, 4, 16], F32)
    gam = sb("gam", [128, 4, 16], F32)
    gamneg = sb("gamneg", [128, 4, 16], F32)
    grep = sb("grep", [128, 4, 16], F32)
    eKD = sb("eKD", [128, 4, 16], F32)
    tcol16 = sb("tcol16", [128, 16], F32)
    Dm = sb("Dm", [128, 4, 128], F32)
    DbG = sb("DbG", [128, HV, 128], BF16)
    DbM = sb("DbM", [128, HV, 128], BF16)
    grq = sb("grq", [128, 16], F32)
    lrow = sb("lrow", [1, 128], F32)
    khatT = sb("khatT", [128, HQ, 128], BF16)
    Akk = sb("Akk", [128, HQ, 128], BF16)
    Akq = sb("Akq", [128, HQ, 128], BF16)
    Vt_all = sb("Vt_all", [128, HV, 128], BF16)
    NSL = 2
    Gbuf = [[sb(f"Gc{s}", [128, 256], BF16), sb(f"Gn{s}", [128, 256], BF16)]
            for s in range(NSL)]
    xch = [sb(f"xch{s}", [128, 2, 128], BF16) for s in range(NSL)]
    Mp = [sb(f"Mp{s}", [128, 128], BF16) for s in range(NSL)]
    Rt = [sb(f"Rt{s}", [128, 128], BF16) for s in range(NSL)]
    Ub = [sb(f"Ub{s}", [128, 128], BF16) for s in range(NSL)]
    Kdec = [sb(f"Kdec{s}", [128, 128], BF16) for s in range(NSL)]
    g1t = [sb(f"g1t{s}", [128, 128], BF16) for s in range(NSL)]
    ea_f = [sb(f"ea{s}", [128, 128], F32) for s in range(NSL)]
    O_f = [sb(f"Of{s}", [128, 128], F32) for s in range(NSL)]
    osq = [sb(f"osq{s}", [128, 128], F32) for s in range(NSL)]
    ossq = [sb(f"ossq{s}", [128, 1], F32) for s in range(NSL)]
    orf = [sb(f"orf{s}", [128, 1], F32) for s in range(NSL)]
    On_b = [sb(f"Onb{s}", [128, 128], BF16) for s in range(NSL)]
    wpool = sb("wpool", [128, 12288], BF16)

    def wslot16(i):
        return wpool[:, i * 2048:(i + 1) * 2048].rearrange(
            "p (a b) -> p a b", b=128)

    def wslot43(i):
        return wpool[:, i * 6144:i * 6144 + NIC * 128].rearrange(
            "p (a b) -> p a b", b=128)

    def wgran16(i):
        return [bm("wp", i)]

    def wgran43(i):
        return [bm("wp", 3 * i), bm("wp", 3 * i + 1), bm("wp", 3 * i + 2)]

    P.op("V", lambda: nc.vector.memset(S32[:].rearrange("p a b -> p (a b)"), 0.0),
         writes=[bm("S32", h) for h in range(HV)])
    P.op("V", lambda: nc.vector.memset(Sbf[:].rearrange("p a b -> p (a b)"), 0.0),
         writes=[bm("Sbf", h) for h in range(HV)])
    P.op("V", lambda: nc.vector.memset(tails[:].rearrange("p a b -> p (a b)"), 0.0),
         writes=[bm("tails", oc) for oc in range(NOC)])
    P.op("V", lambda: nc.vector.memset(ones16[0:16, :], 1.0), writes=[bm("ones16")])

    def stream16(scr_ap, scr_buf, sl, tag):
        P.op("SY", (lambda a=scr_ap, sl=sl: nc.sync.dma_start(
            wslot16(sl), a)),
             reads=[scr_buf], writes=wgran16(sl), dma=True, tag=tag)

    for ti in range(n_tiles):
        t0 = ti * TT

        for tc in range(4):
            r0 = t0 + tc * 128
            lnb = xst if tc % 2 == 0 else xst2
            lk = "xst" if tc % 2 == 0 else "xst2"
            P.op("SY", (lambda r0=r0, lnb=lnb: nc.sync.dma_start(
                lnb[:], x_d[r0:r0 + 128, :])),
                 writes=[bm(lk)], dma=True, tag="lnld")
            P.op("V", (lambda lnb=lnb: nc.vector.reduce_sum(
                mcol[:], lnb[:], axis=AX.X)),
                 reads=[bm(lk)], writes=[bm("mcol")], tag="lnred")
            P.op("V", lambda: nc.vector.tensor_scalar(mcol[:], mcol[:], 1.0 / D,
                                                      None, ALU.mult),
                 reads=[bm("mcol")], writes=[bm("mcol")])
            P.op("SC", (lambda lnb=lnb: nc.scalar.activation(
                xntok[:], lnb[:], AF.Square, accum_out=s2col[:])),
                 reads=[bm(lk)], writes=[bm("xntok"), bm("s2col")], tag="lnsq")
            P.op("V", lambda: nc.vector.tensor_mul(vtmp1[:], mcol[:], mcol[:]),
                 reads=[bm("mcol")], writes=[bm("vt1")])
            P.op("V", lambda: nc.vector.tensor_scalar(s2col[:], s2col[:], 1.0 / D,
                                                      None, ALU.mult),
                 reads=[bm("s2col")], writes=[bm("s2col")])
            P.op("V", lambda: nc.vector.tensor_sub(s2col[:], s2col[:], vtmp1[:]),
                 reads=[bm("s2col"), bm("vt1")], writes=[bm("s2col")])
            P.op("SC", lambda: nc.scalar.activation(rstd[:], s2col[:], AF.Ln,
                                                    bias=eps5[:]),
                 reads=[bm("s2col"), bm("eps5")], writes=[bm("rstd")])
            P.op("SC", lambda: nc.scalar.activation(rstd[:], rstd[:], AF.Exp,
                                                    scale=-0.5),
                 reads=[bm("rstd")], writes=[bm("rstd")])
            P.op("V", (lambda lnb=lnb: nc.vector.tensor_scalar(
                xntok[:], lnb[:], mcol[:], rstd[:], ALU.subtract, ALU.mult)),
                 reads=[bm(lk), bm("mcol"), bm("rstd")], writes=[bm("xntok")],
                 tag="lnn")
            for half in range(2):
                k0 = half * 8
                for kc in range(k0, k0 + 8):
                    P.op("PE", (lambda kc=kc, k0=k0, bi=half:
                                nc.tensor.transpose(
                                    BT[bi][:, (kc - k0) * 128:(kc - k0 + 1) * 128],
                                    xntok[:, kc * 128:(kc + 1) * 128], m_idb[:])),
                         reads=[bm("xntok"), bm("m_idb")], writes=[bm("BT", half)],
                         tag="lntr")
                P.op("V", (lambda k0=k0, bi=half, tc=tc: nc.vector.tensor_copy(
                    xnT[:, k0:k0 + 8, tc * 128:(tc + 1) * 128],
                    BT[bi][:].rearrange("p (a b) -> p a b", a=8))),
                     reads=[bm("BT", half)],
                     writes=[bm("xnT", k) for k in range(k0, k0 + 8)], tag="lnev")

        for kc in range(NKC):
            rds = ([bm("wbaT")] + [bm("xnT", k2) for k2 in range(NKC)]
                   if kc == 0 else [])
            P.op("PE", (lambda kc=kc: nc.tensor.matmul(
                B[5][0:64, :], wbaT[:, kc, :], xnT[:, kc, :],
                start=(kc == 0), stop=(kc == NKC - 1))),
                 reads=rds,
                 reads_nodep=([bm("wbaT"), bm("xnT", kc)] if kc else []),
                 writes=[bm("B", 5)], tag="bamm")
        P.op("SC", lambda: nc.scalar.activation(beta_f[0:16, :], B[5][0:16, :],
                                                AF.Exp, scale=-1.0),
             reads=[bm("B", 5)], writes=[bm("beta_f")], tag="basig")
        P.op("V", lambda: nc.vector.tensor_scalar(beta_f[0:16, :], beta_f[0:16, :],
                                                  1.0, None, ALU.add),
             reads=[bm("beta_f")], writes=[bm("beta_f")])
        P.op("V", lambda: nc.vector.reciprocal(beta_f[0:16, :], beta_f[0:16, :]),
             reads=[bm("beta_f")], writes=[bm("beta_f")])
        P.op("SC", lambda: nc.scalar.activation(sp_t[0:16, :], B[5][32:48, :],
                                                AF.Exp, bias=dtb[:]),
             reads=[bm("B", 5), bm("dtb")], writes=[bm("sp_t")], tag="baexp")
        P.op("SC", lambda: nc.scalar.activation(sp_t[0:16, :], sp_t[0:16, :],
                                                AF.Ln, bias=1.0),
             reads=[bm("sp_t")], writes=[bm("sp_t")])
        P.op("V", lambda: nc.vector.tensor_scalar(la_f[0:16, :], sp_t[0:16, :],
                                                  negA[:], None, ALU.mult),
             reads=[bm("sp_t"), bm("negA")], writes=[bm("la_f")], tag="bala")

        for c in range(4):
            cs = slice(c * 128, (c + 1) * 128)
            P.op("V", (lambda cs=cs: nc.vector.tensor_tensor_scan(
                sp_t[0:16, cs], ones16[0:16, :], la_f[0:16, cs], 0.0,
                ALU.mult, ALU.add)),
                 reads=[bm("la_f"), bm("ones16")], writes=[bm("sp_t")], tag="scanL")
            rev = slice((c + 1) * 128 - 1, (c * 128 - 1) if c > 0 else None, -1)
            P.op("V", (lambda rev=rev: nc.vector.tensor_tensor_scan(
                suf_f[0:16, rev], ones16[0:16, :], la_f[0:16, rev], 0.0,
                ALU.mult, ALU.add)),
                 reads=[bm("la_f"), bm("ones16")], writes=[bm("suf_f")], tag="scanS")
            for (src_t, skey, dst, dkey) in ((sp_t, "sp_t", Lt, "Lt"),
                                             (suf_f, "suf_f", sufT, "sufT"),
                                             (la_f, "la_f", laT, "laT"),
                                             (beta_f, "beta_f", betaT, "betaT")):
                P.op("PE", (lambda s=src_t, cs=cs: nc.tensor.transpose(
                    B[4][:, 0:16], s[0:16, cs], m_idf[0:16, 0:16])),
                     reads=[bm(skey), bm("m_idf")], writes=[bm("B", 4)], tag="trL")
                P.op("V", (lambda d=dst, c=c: nc.vector.tensor_copy(d[:, c, :],
                                                                    B[4][:, 0:16])),
                     reads=[bm("B", 4)], writes=[bm(dkey)], tag="trLev")
            P.op("SC", (lambda c=c: nc.scalar.activation(gam[:, c, :], Lt[:, c, :],
                                                         AF.Exp)),
                 reads=[bm("Lt")], writes=[bm("gam")])
            P.op("V", (lambda c=c: nc.vector.tensor_scalar(
                gamneg[:, c, :], gam[:, c, :], -1.0, None, ALU.mult)),
                 reads=[bm("gam")], writes=[bm("gamneg")])
            P.op("V", (lambda c=c: nc.vector.tensor_sub(tcol16[:], sufT[:, c, :],
                                                        laT[:, c, :])),
                 reads=[bm("sufT"), bm("laT")], writes=[bm("tcol16")])
            P.op("SC", (lambda c=c: nc.scalar.activation(eKD[:, c, :], tcol16[:],
                                                         AF.Exp)),
                 reads=[bm("tcol16")], writes=[bm("eKD")])
            P.op("V", (lambda c=c: nc.vector.tensor_add(tcol16[:], tcol16[:],
                                                        Lt[:, c, :])),
                 reads=[bm("tcol16"), bm("Lt")], writes=[bm("tcol16")])
            P.op("SC", (lambda c=c: nc.scalar.activation(grep[:, c, :], tcol16[:],
                                                         AF.Exp)),
                 reads=[bm("tcol16")], writes=[bm("grep")])

        for oc in range(NOC):
            sl = oc % 2
            stream16(s_wq[:, oc, :, :], bm("s_wq", oc), sl, "qkld")
            pi = oc % 2
            for kc in range(NKC):
                rds = (wgran16(sl) + [bm("xnT", k2) for k2 in range(NKC)]
                       if kc == 0 else [])
                P.op("PE", (lambda kc=kc, sl=sl, pi=pi: nc.tensor.matmul(
                    B[pi][:, :], wslot16(sl)[:, kc, :], xnT[:, kc, :],
                    start=(kc == 0), stop=(kc == NKC - 1))),
                     reads=rds, reads_nodep=(wgran16(sl) + [bm("xnT", kc)] if kc else []),
                     writes=[bm("B", pi)], tag="qkmm")
            P.op("V", (lambda oc=oc: nc.vector.tensor_copy(raw_ext[:, 0:3],
                                                           tails[:, oc, :])),
                 reads=[bm("tails", oc)], writes=[bm("raw_ext")], tag="cvtl")
            P.op("V", (lambda pi=pi: nc.vector.tensor_copy(raw_ext[:, 3:515],
                                                           B[pi][:, :])),
                 reads=[bm("B", pi)], writes=[bm("raw_ext")], tag="cvev")
            P.op("V", (lambda oc=oc: nc.vector.tensor_copy(tails[:, oc, :],
                                                           raw_ext[:, 512:515])),
                 reads=[bm("raw_ext")], writes=[bm("tails", oc)], tag="cvnt")
            P.op("V", (lambda oc=oc: nc.vector.tensor_scalar(
                cvt_b[:], raw_ext[:, 3:515], convw[:, oc, 3:4], None, ALU.mult)),
                 reads=[bm("raw_ext"), bm("convw")], writes=[bm("cvt")], tag="cv3")
            for k in (2, 1, 0):
                P.op("V", (lambda oc=oc, k=k: nc.vector.scalar_tensor_tensor(
                    cvt_b[:], raw_ext[:, k:k + 512], convw[:, oc, k:k + 1],
                    cvt_b[:], ALU.mult, ALU.add)),
                     reads=[bm("raw_ext"), bm("convw"), bm("cvt")],
                     writes=[bm("cvt")], tag="cvk")
            P.op("SC", (lambda oc=oc: nc.scalar.activation(cooked[:, oc, :],
                                                           cvt_b[:], AF.Silu)),
                 reads=[bm("cvt")], writes=[bm("cooked", oc)], tag="cvsl")
            if oc < 16:
                P.op("V", (lambda oc=oc: nc.vector.tensor_mul(
                    cvt_b[:], cooked[:, oc, :], cooked[:, oc, :])),
                     reads=[bm("cooked", oc)], writes=[bm("cvt")], tag="l2sq")
                P.op("PE", lambda: nc.tensor.matmul(
                    B[4][0:1, :], ones_cb[:], cvt_b[:], start=True, stop=True),
                     reads=[bm("cvt"), bm("ones_cb")], writes=[bm("B", 4)],
                     tag="l2mm")
                P.op("SC", lambda: nc.scalar.activation(
                    rrow_f[0:1, :], B[4][0:1, :], AF.Ln, bias=eps6[0:1, :]),
                     reads=[bm("B", 4), bm("eps6")], writes=[bm("rrow")], tag="l2ln")
                P.op("SC", lambda: nc.scalar.activation(
                    rrow_b[0:1, :], rrow_f[0:1, :], AF.Exp, scale=-0.5),
                     reads=[bm("rrow")], writes=[bm("rrowb")], tag="l2ex")
                for c in range(4):
                    P.op("PE", (lambda c=c: nc.tensor.matmul(
                        B[4][:, c + 2:c + 3],
                        rrow_b[0:1, c * 128:(c + 1) * 128], one_b[0:1, :],
                        start=True, stop=True)),
                         reads=[bm("rrowb"), bm("one_b")], writes=[bm("B", 4)],
                         tag="rcm")
                if oc < 8:
                    P.op("V", (lambda oc=oc: nc.vector.tensor_scalar(
                        rqcols[:, oc, :], B[4][:, 2:6], DKSC, None, ALU.mult)),
                         reads=[bm("B", 4)], writes=[bm("rqc", oc)], tag="rqcv")
                else:
                    P.op("V", (lambda oc=oc: nc.vector.tensor_copy(
                        rkcols[:, oc - 8, :], B[4][:, 2:6])),
                         reads=[bm("B", 4)], writes=[bm("rkc", oc - 8)], tag="rkcv")
                    P.op("PE", (lambda: nc.tensor.matmul(
                        B[5][:, :], ones_rb[0:1, :], rrow_b[0:1, :],
                        start=True, stop=True)),
                         reads=[bm("rrowb"), bm("ones_rb")], writes=[bm("B", 5)],
                         tag="krep")
                    P.op("V", (lambda oc=oc: nc.vector.tensor_mul(
                        cooked[:, oc, :], cooked[:, oc, :], B[5][:, :])),
                         reads=[bm("cooked", oc), bm("B", 5)],
                         writes=[bm("cooked", oc)], tag="khat")

        for hc in range(NKC):
            sl = hc % 2
            stream16(s_wg[:, hc, :, :], bm("s_wg", hc), sl, "gld")
            pi = hc % 2
            for kc in range(NKC):
                rds = (wgran16(sl) + [bm("xnT", k2) for k2 in range(NKC)]
                       if kc == 0 else [])
                P.op("PE", (lambda kc=kc, sl=sl, pi=pi: nc.tensor.matmul(
                    B[pi][:, :], wslot16(sl)[:, kc, :], xnT[:, kc, :],
                    start=(kc == 0), stop=(kc == NKC - 1))),
                     reads=rds, reads_nodep=(wgran16(sl) + [bm("xnT", kc)] if kc else []),
                     writes=[bm("B", pi)], tag="gmm")
            P.op("SC", (lambda hc=hc, pi=pi: nc.scalar.activation(
                gateb[:, hc, :], B[pi][:, :], AF.Silu)),
                 reads=[bm("B", pi)], writes=[bm("gateb", hc)], tag="gsl")

        def gu_ops(ic):
            slg = ic % 2
            slu = 2 + ic % 2
            pg, pu = 0, 1
            hs = ic % 2
            yield ("SY", (lambda ic=ic, slg=slg: nc.sync.dma_start(
                wslot16(slg),
                ag_g[ic, :, :].rearrange("p (a b) -> p a b", b=128))),
                [bm("s_fg", ic)], wgran16(slg), "fgld", True)
            yield ("SY", (lambda ic=ic, slu=slu: nc.sync.dma_start(
                wslot16(slu),
                ag_u[ic, :, :].rearrange("p (a b) -> p a b", b=128))),
                [bm("s_fu", ic)], wgran16(slu), "fuld", True)
            for kc in range(NKC):
                rds = (wgran16(slg) + [bm("xnT", k2) for k2 in range(NKC)]
                       if kc == 0 else [])
                nod = [] if kc == 0 else wgran16(slg) + [bm("xnT", kc)]
                yield ("PE", (lambda kc=kc, slg=slg: nc.tensor.matmul(
                    B[0][:, :], wslot16(slg)[:, kc, :], xnT[:, kc, :],
                    start=(kc == 0), stop=(kc == NKC - 1))),
                    rds, [bm("B", 0)], "fgmm", False, nod)
            for kc in range(NKC):
                rds = (wgran16(slu) + [bm("xnT", k2) for k2 in range(NKC)]
                       if kc == 0 else [])
                nod = [] if kc == 0 else wgran16(slu) + [bm("xnT", kc)]
                yield ("PE", (lambda kc=kc, slu=slu: nc.tensor.matmul(
                    B[1][:, :], wslot16(slu)[:, kc, :], xnT[:, kc, :],
                    start=(kc == 0), stop=(kc == NKC - 1))),
                    rds, [bm("B", 1)], "fumm", False, nod)
            yield ("SC", (lambda: nc.scalar.activation(sg_b[:], B[0][:, :],
                                                       AF.Silu)),
                [bm("B", 0)], [bm("sg")], "fsl", False)
            yield ("V", (lambda hs=hs: nc.vector.tensor_copy(
                ub_t[hs][:], B[1][:, :])),
                [bm("B", 1)], [bm("ubt", hs)], "uev", False)
            yield ("V", (lambda hs=hs: nc.vector.tensor_mul(
                hstage[hs][:], sg_b[:], ub_t[hs][:])),
                [bm("sg"), bm("ubt", hs)], [bm("hstage", hs)], "fmul", False)
            yield ("SY", (lambda ic=ic, hs=hs: nc.sync.dma_start(
                s_hact[ic, :, :], hstage[hs][:])),
                [bm("hstage", hs)], [bm("s_hact", ic)], "fhst", True)

        def gu_chain():
            for ic in range(NIC):
                for t in gu_ops(ic):
                    yield t

        gu_gen = gu_chain()
        gu_done = [False]

        def pump_gu_one():
            if gu_done[0]:
                return
            try:
                t = next(gu_gen)
                nod = t[6] if len(t) > 6 else ()
                P.op(t[0], t[1], reads=t[2], writes=t[3], dma=t[5], tag=t[4],
                     reads_nodep=nod)
            except StopIteration:
                gu_done[0] = True

        def pump_gu_rest():
            while not gu_done[0]:
                pump_gu_one()

        for c in range(4):
            cs = slice(c * 128, (c + 1) * 128)
            for q4 in range(4):
                bi = q4 % 2
                for j in range(4):
                    hv = q4 * 4 + j
                    P.op("PE", (lambda hv=hv, j=j, bi=bi, cs=cs:
                                nc.tensor.transpose(BT[bi][:, j * 128:(j + 1) * 128],
                                                    cooked[:, 16 + hv, cs],
                                                    m_idb[:])),
                         reads=[bm("cooked", 16 + hv), bm("m_idb")],
                         writes=[bm("BT", bi)], tag="vtr")
                P.op("V", (lambda q4=q4, bi=bi: nc.vector.tensor_copy(
                    Vt_all[:, q4 * 4:(q4 + 1) * 4, :].rearrange("p a b -> p (a b)"),
                    BT[bi][:, 0:512])),
                     reads=[bm("BT", bi)], writes=[bm("Vt", q4)], tag="vtev")
            for qh in range(HQ):
                P.op("PE", (lambda qh=qh, cs=cs: nc.tensor.transpose(
                    BT[0][:, 0:128], cooked[:, 8 + qh, cs], m_idb[:])),
                     reads=[bm("cooked", 8 + qh), bm("m_idb")], writes=[bm("BT", 0)],
                     tag="ktr")
                P.op("V", (lambda qh=qh: nc.vector.tensor_copy(khatT[:, qh, :],
                                                               BT[0][:, 0:128])),
                     reads=[bm("BT", 0)], writes=[bm("khatT", qh)], tag="ktev")
                P.op("PE", (lambda qh=qh, cs=cs: nc.tensor.matmul(
                    B[2][:, 0:128], cooked[:, 8 + qh, cs], cooked[:, 8 + qh, cs],
                    start=True, stop=True)),
                     reads=[bm("cooked", 8 + qh)], writes=[bm("B", 2)], tag="akk")
                P.op("SC", (lambda qh=qh: nc.scalar.copy(Akk[:, qh, :],
                                                         B[2][:, 0:128])),
                     reads=[bm("B", 2)], writes=[bm("Akk", qh)], tag="akkev")
                P.op("PE", (lambda qh=qh, cs=cs: nc.tensor.matmul(
                    B[2][:, 128:256], cooked[:, 8 + qh, cs], cooked[:, qh, cs],
                    start=True, stop=True)),
                     reads=[bm("cooked", 8 + qh), bm("cooked", qh)], writes=[bm("B", 2)],
                     tag="akq")
                P.op("SC", (lambda qh=qh: nc.scalar.copy(Akq[:, qh, :],
                                                         B[2][:, 128:256])),
                     reads=[bm("B", 2)], writes=[bm("Akq", qh)], tag="akqev")
            for hg in range(4):
                for hh in range(4):
                    hv = hg * 4 + hh
                    P.op("PE", (lambda hv=hv, c=c: nc.tensor.transpose(
                        B[4][0:1, 128:256], Lt[:, c, hv:hv + 1], m_idf[:])),
                         reads=[bm("Lt"), bm("m_idf")], writes=[bm("B", 4)],
                         tag="ltr")
                    P.op("V", lambda: nc.vector.tensor_copy(lrow[0:1, :],
                                                            B[4][0:1, 128:256]),
                         reads=[bm("B", 4)], writes=[bm("lrow")], tag="lrev")
                    P.op("G", (lambda hh=hh: nc.gpsimd.partition_broadcast(
                        Dm[:, hh, :], lrow[0:1, :])),
                         reads=[bm("lrow")], writes=[bm("Dm", hh)], tag="lbc")
                    P.op("V", (lambda hh=hh, hv=hv, c=c: nc.vector.tensor_scalar(
                        Dm[:, hh, :], Dm[:, hh, :], Lt[:, c, hv:hv + 1], 0.0,
                        ALU.subtract, ALU.min)),
                         reads=[bm("Dm", hh), bm("Lt")], writes=[bm("Dm", hh)],
                         tag="dts")
                P.op("SC", (lambda hg=hg: nc.scalar.activation(
                    DbM[:, hg * 4:(hg + 1) * 4, :].rearrange("p a b -> p (a b)"),
                    Dm[:].rearrange("p a b -> p (a b)"), AF.Exp)),
                     reads=[bm("Dm", h) for h in range(4)],
                     writes=[bm("DbM", hg * 4 + h) for h in range(4)], tag="dexp")
                P.op("V", (lambda hg=hg: nc.vector.tensor_mul(
                    DbG[:, hg * 4:(hg + 1) * 4, :].rearrange("p a b -> p (a b)"),
                    DbM[:, hg * 4:(hg + 1) * 4, :].rearrange("p a b -> p (a b)"),
                    m_sneg8[:, 0:512])),
                     reads=[bm("DbM", hg * 4 + h) for h in range(4)] + [bm("m_sneg8")],
                     writes=[bm("DbG", hg * 4 + h) for h in range(4)], tag="dbg")
                P.op("V", (lambda hg=hg: nc.vector.tensor_mul(
                    DbM[:, hg * 4:(hg + 1) * 4, :].rearrange("p a b -> p (a b)"),
                    DbM[:, hg * 4:(hg + 1) * 4, :].rearrange("p a b -> p (a b)"),
                    m_incl8[:, 0:512])),
                     reads=[bm("DbM", hg * 4 + h) for h in range(4)] + [bm("m_incl8")],
                     writes=[bm("DbM", hg * 4 + h) for h in range(4)], tag="dbm")
            for hv in range(HV):
                P.op("V", (lambda hv=hv, c=c: nc.vector.tensor_mul(
                    grq[:, hv:hv + 1], gam[:, c, hv:hv + 1],
                    rqcols[:, hv // 2, c:c + 1])),
                     reads=[bm("gam"), bm("rqc", hv // 2)], writes=[bm("grq", hv)],
                     tag="grq")

            def head_ops(hv, s, c, cs):
                qh = hv // 2
                KQ = B[2] if s == 0 else B[4]
                APb = KQ
                SQ = B[3] if s == 0 else B[5]
                BTs = BT[1] if s == 0 else BT[0]
                kq, sq, bts = ((("B", 2), ("B", 3), ("BT", 1)) if s == 0
                               else (("B", 4), ("B", 5), ("BT", 0)))
                apb = kq
                G2 = Gbuf[s]
                yield ("PE", (lambda: nc.tensor.matmul(
                    KQ[:, 0:128], cooked[:, 8 + qh, cs], Sbf[:, hv, :],
                    start=True, stop=True)),
                    [bm("cooked", 8 + qh), bm("Sbf", hv)], [bm(*kq)], "ks0")
                yield ("PE", (lambda: nc.tensor.matmul(
                    KQ[:, 128:256], cooked[:, qh, cs], Sbf[:, hv, :],
                    start=True, stop=True)),
                    [bm("cooked", qh), bm("Sbf", hv)], [bm(*kq)], "qs0")
                yield ("V", (lambda: nc.vector.scalar_tensor_tensor(
                    G2[0][:, 0:128], DbG[:, hv, :], betaT[:, c, hv:hv + 1],
                    Akk[:, qh, :], ALU.mult, ALU.mult)),
                    [bm("DbG", hv), bm("betaT"), bm("Akk", qh)], [bm("G", s, 0)],
                    "gb2")
                yield ("V", (lambda: nc.vector.tensor_mul(
                    Mp[s][:], DbM[:, hv, :], Akq[:, qh, :])),
                    [bm("DbM", hv), bm("Akq", qh)], [bm("Mp", s)], "mb2")
                yield ("V", (lambda: nc.vector.scalar_tensor_tensor(
                    Rt[s][:], KQ[:, 0:128], gamneg[:, c, hv:hv + 1],
                    Vt_all[:, hv, :], ALU.mult, ALU.add)),
                    [bm(*kq), bm("gamneg"), bm("Vt", hv // 4)], [bm("Rt", s)],
                    "rt")
                yield ("PE", (lambda: nc.tensor.transpose(
                    BTs[:, 0:128], G2[0][:, 0:128], m_idb[:])),
                    [bm("G", s, 0), bm("m_idb")], [bm(*bts)], "g1tr")
                yield ("V", (lambda: nc.vector.tensor_copy(
                    G2[0][:, 128:256], BTs[:, 0:128])),
                    [bm(*bts)], [bm("G", s, 0)], "g1ev")
                for j in range(7):
                    gi = j % 2
                    xi = j % 2
                    xap = Rt[s][:, :] if j == 0 else xch[s][:, (j - 1) % 2, :]
                    xbuf = bm("Rt", s) if j == 0 else bm("xch", s, (j - 1) % 2)
                    if j < 6:
                        yield ("PE", (lambda gi=gi: nc.tensor.matmul(
                            SQ[:, 0:128], G2[gi][:, 128:256], G2[gi][:, 0:128],
                            start=True, stop=True)),
                            [bm("G", s, gi)], [bm(*sq)], "ssq1")
                        yield ("PE", (lambda gi=gi: nc.tensor.matmul(
                            SQ[:, 128:256], G2[gi][:, 0:128], G2[gi][:, 128:256],
                            start=True, stop=True)),
                            [bm("G", s, gi)], [bm(*sq)], "ssq2")
                    yield ("PE", (lambda gi=gi, x=xap: nc.tensor.matmul(
                        APb[:, 0:128], G2[gi][:, 0:128], x,
                        start=True, stop=True)),
                        [bm("G", s, gi), xbuf], [bm(*apb)], "sapg")
                    if j < 6:
                        yield ("SC", (lambda gi=gi: nc.scalar.copy(
                            G2[1 - gi][:, 0:256], SQ[:, 0:256])),
                            [bm(*sq)], [bm("G", s, 1 - gi)], "ssqev")
                    yield ("V", (lambda xi=xi, x=xap: nc.vector.tensor_add(
                        xch[s][:, xi, :], APb[:, 0:128], x)),
                        [bm(*apb), xbuf], [bm("xch", s, xi)], "sapev")
                yield ("SC", (lambda: nc.scalar.activation(
                    Ub[s][:], xch[s][:, 0, :], AF.Copy,
                    scale=betaT[:, c, hv:hv + 1])),
                    [bm("xch", s, 0), bm("betaT")], [bm("Ub", s)], "ub")
                yield ("PE", (lambda: nc.tensor.matmul(
                    APb[:, 0:128], Mp[s][:], Ub[s][:], start=True, stop=True)),
                    [bm("Mp", s), bm("Ub", s)], [bm(*apb)], "oin")
                yield ("SC", (lambda: nc.scalar.activation(
                    ea_f[s][:], KQ[:, 128:256], AF.Copy,
                    scale=grq[:, hv:hv + 1])),
                    [bm(*kq), bm("grq", hv)], [bm("ea", s)], "oea")
                yield ("V", (lambda: nc.vector.scalar_tensor_tensor(
                    O_f[s][:], APb[:, 0:128], rqcols[:, qh, c:c + 1], ea_f[s][:],
                    ALU.mult, ALU.add)),
                    [bm(*apb), bm("rqc", qh), bm("ea", s)], [bm("Of", s)], "osum")
                yield ("SC", (lambda: nc.scalar.activation(
                    osq[s][:], O_f[s][:], AF.Square, accum_out=ossq[s][:])),
                    [bm("Of", s)], [bm("osq", s), bm("ossq", s)], "orsq")
                yield ("SC", (lambda: nc.scalar.activation(
                    orf[s][:], ossq[s][:], AF.Ln, scale=1.0 / 128,
                    bias=eps6[:])),
                    [bm("ossq", s), bm("eps6")], [bm("orf", s)], "orln")
                yield ("SC", (lambda: nc.scalar.activation(
                    orf[s][:], orf[s][:], AF.Exp, scale=-0.5)),
                    [bm("orf", s)], [bm("orf", s)], "orex")
                yield ("SC", (lambda: nc.scalar.activation(
                    On_b[s][:], O_f[s][:], AF.Copy, scale=orf[s][:])),
                    [bm("Of", s), bm("orf", s)], [bm("Onb", s)], "onb")
                yield ("PE", (lambda: nc.tensor.transpose(
                    BTs[:, 128:256], On_b[s][:], m_idb[:])),
                    [bm("Onb", s), bm("m_idb")], [bm(*bts)], "ctr")
                yield ("V", (lambda: nc.vector.scalar_tensor_tensor(
                    gateb[:, hv, cs], gateb[:, hv, cs], rmsw[:],
                    BTs[:, 128:256], ALU.mult, ALU.mult)),
                    [bm("gateb", hv), bm("rmsw"), bm(*bts)],
                    [bm("gateb", hv)], "gmul")
                yield ("SC", (lambda: nc.scalar.activation(
                    Kdec[s][:], khatT[:, qh, :], AF.Copy,
                    scale=eKD[:, c, hv:hv + 1])),
                    [bm("khatT", qh), bm("eKD")], [bm("Kdec", s)], "kdec")
                yield ("PE", (lambda: nc.tensor.matmul(
                    APb[:, 128:256], Kdec[s][:], Ub[s][:],
                    start=True, stop=True)),
                    [bm("Kdec", s), bm("Ub", s)], [bm(*apb)], "snmm")
                yield ("V", (lambda: nc.vector.scalar_tensor_tensor(
                    S32[:, hv, :], S32[:, hv, :], grep[:, c, hv:hv + 1],
                    APb[:, 128:256], ALU.mult, ALU.add)),
                    [bm("S32", hv), bm("grep"), bm(*apb)], [bm("S32", hv)],
                    "sup")
                yield ("V", (lambda: nc.vector.tensor_copy(
                    Sbf[:, hv, :], S32[:, hv, :])),
                    [bm("S32", hv)], [bm("Sbf", hv)], "sbf")

            state = {"next": 0, "lanes": [None, None]}

            def refill(s):
                if state["next"] < HV:
                    state["lanes"][s] = head_ops(state["next"], s, c, cs)
                    state["next"] += 1
                else:
                    state["lanes"][s] = None

            def step(s):
                g = state["lanes"][s]
                if g is None:
                    return False
                try:
                    (e, em, rd, wr, tg) = next(g)
                    P.op(e, em, reads=rd, writes=wr, tag=tg)
                    return True
                except StopIteration:
                    refill(s)
                    return step(s)

            refill(0)
            refill(1)
            for _ in range(38):
                step(0)
            while state["lanes"][0] is not None or state["lanes"][1] is not None:
                step(0)
                step(1)
                pump_gu_one()

        pump_gu_rest()

        for hc in range(NKC):
            sl = hc % 2
            stream16(s_wo[:, hc, :, :], bm("s_wo", hc), sl, "old")
            pi = hc % 2
            for kc in range(NKC):
                rds = (wgran16(sl) + [bm("gateb", k2) for k2 in range(NKC)]
                       if kc == 0 else [])
                P.op("PE", (lambda kc=kc, sl=sl, pi=pi: nc.tensor.matmul(
                    B[pi][:, :], wslot16(sl)[:, kc, :], gateb[:, kc, :],
                    start=(kc == 0), stop=(kc == NKC - 1))),
                     reads=rds, reads_nodep=(wgran16(sl) + [bm("gateb", kc)] if kc else []),
                     writes=[bm("B", pi)], tag="omm")
            P.op("V", (lambda hc=hc, pi=pi: nc.vector.tensor_copy(
                accb[:, hc, :], B[pi][:, :])),
                 reads=[bm("B", pi)], writes=[bm("accb", hc)], tag="oev")

        for (h0, h1) in ((0, 6), (6, 12), (12, 16)):
            nb = h1 - h0
            for oc in range(NIC):
                P.op("SY", (lambda oc=oc, r=oc % 4: nc.sync.dma_start(
                    hring[r][:], s_hact[oc, :, :])),
                     reads=[bm("s_hact", oc)], writes=[bm("hring", oc % 4)],
                     dma=True, tag="fdhl")
                wsl = oc % 2
                P.op("SY", (lambda oc=oc, wsl=wsl, h0=h0, h1=h1, nb=nb:
                            nc.sync.dma_start(
                    wpool[:, wsl * 6144:wsl * 6144 + nb * 128].rearrange(
                        "p (a b) -> p a b", b=128),
                    ag_d[h0:h1, :, oc * 128:(oc + 1) * 128]
                        .transpose([1, 0, 2]))),
                     reads=[bm("s_fd", hh) for hh in range(h0, h1)],
                     writes=wgran43(wsl), dma=True, tag="fdwl")
                for k in range(nb):
                    P.op("PE", (lambda oc=oc, wsl=wsl, k=k: nc.tensor.matmul(
                        B[k][:, :],
                        wpool[:, wsl * 6144 + k * 128:wsl * 6144 + (k + 1) * 128],
                        hring[oc % 4][:], start=(oc == 0),
                        stop=(oc == NIC - 1))),
                         reads=wgran43(wsl) + [bm("hring", oc % 4)],
                         writes=[bm("B", k)], tag="fdmm")
            for k in range(nb):
                P.op("V", (lambda hc=h0 + k, k=k: nc.vector.tensor_add(
                    accb[:, hc, :], accb[:, hc, :], B[k][:, :])),
                     reads=[bm("accb", h0 + k), bm("B", k)],
                     writes=[bm("accb", h0 + k)], tag="fdev")

        for tc in range(4):
            r0 = t0 + tc * 128
            P.op("SY", (lambda r0=r0: nc.sync.dma_start(xout[:], x_d[r0:r0 + 128, :])),
                 writes=[bm("xout")], dma=True, tag="xold")
            for half in range(2):
                k0 = half * 8
                for hc in range(k0, k0 + 8):
                    P.op("PE", (lambda hc=hc, k0=k0, bi=half, tc=tc:
                                nc.tensor.transpose(
                                    BT[bi][:, (hc - k0) * 128:(hc - k0 + 1) * 128],
                                    accb[:, hc, tc * 128:(tc + 1) * 128], m_idb[:])),
                         reads=[bm("accb", hc), bm("m_idb")],
                         writes=[bm("BT", half)], tag="otr")
                P.op("V", (lambda k0=k0, bi=half: nc.vector.tensor_add(
                    xout[:, k0 * 128:(k0 + 8) * 128],
                    xout[:, k0 * 128:(k0 + 8) * 128], BT[bi][:, 0:1024])),
                     reads=[bm("BT", half), bm("xout")], writes=[bm("xout")],
                     tag="oadd")
            P.op("SY", (lambda r0=r0: nc.sync.dma_start(out_d[r0:r0 + 128, :],
                                                        xout[:])),
                 reads=[bm("xout")], writes=[bm("outd", r0)], dma=True, tag="ost")

    P.finalize(nc, sems, lane_sems)
    st.close()
    return nc


def _consts():
    ii = np.arange(128)
    return {
        "cid_bf": np.eye(128, dtype=np.float32).astype(ml_dtypes.bfloat16),
        "cid_f": np.eye(128, dtype=np.float32),
        "cm_incl": (ii[:, None] <= ii[None, :]).astype(np.float32)
                    .astype(ml_dtypes.bfloat16),
        "cm_sneg": (-(ii[:, None] < ii[None, :]).astype(np.float32))
                    .astype(ml_dtypes.bfloat16),
        "cm_incl8": np.tile((ii[:, None] <= ii[None, :]).astype(np.float32),
                            (1, 8)).astype(ml_dtypes.bfloat16),
        "cm_sneg8": np.tile(-(ii[:, None] < ii[None, :]).astype(np.float32),
                            (1, 8)).astype(ml_dtypes.bfloat16),
    }


def make_in_maps(L0_attn_x, L0_attn_w_qkv, L0_attn_w_gate, L0_attn_w_beta,
                 L0_attn_w_alpha, L0_attn_log_A, L0_attn_dt_bias,
                 L0_attn_conv_w, L0_attn_rms_w, L0_attn_w_out, L0_ffn_lut,
                 L0_ffn_gate_idx, L0_ffn_up_idx, L0_ffn_down_idx):
    x = np.ascontiguousarray(np.asarray(L0_attn_x, dtype=np.float32))
    shared = {
        "w_qkv": np.ascontiguousarray(np.asarray(L0_attn_w_qkv, np.float32)),
        "w_gate": np.ascontiguousarray(np.asarray(L0_attn_w_gate, np.float32)),
        "w_out": np.ascontiguousarray(np.asarray(L0_attn_w_out, np.float32)),
        "w_ba": np.ascontiguousarray(np.concatenate(
            [np.asarray(L0_attn_w_beta, np.float32),
             np.asarray(L0_attn_w_alpha, np.float32)], axis=0)),
        "conv_w": np.ascontiguousarray(
            np.asarray(L0_attn_conv_w, np.float32).reshape(4096, 4)),
        "log_A": np.ascontiguousarray(
            np.asarray(L0_attn_log_A, np.float32).reshape(16, 1)),
        "dt_bias": np.ascontiguousarray(
            np.asarray(L0_attn_dt_bias, np.float32).reshape(16, 1)),
        "rms_w": np.ascontiguousarray(
            np.asarray(L0_attn_rms_w, np.float32).reshape(128, 1)),
    }
    shared.update(_consts())
    gidx = np.asarray(L0_ffn_gate_idx, np.int32)
    uidx = np.asarray(L0_ffn_up_idx, np.int32)
    didx = np.asarray(L0_ffn_down_idx, np.int32)
    gpad = np.concatenate([gidx, np.tile(gidx[-128:], (5, 1))], axis=0)
    upad = np.concatenate([uidx, np.tile(uidx[-128:], (5, 1))], axis=0)
    maps = []
    for b in range(x.shape[0]):
        maps.append(dict(
            shared, x=np.ascontiguousarray(x[b]),
            my_g=np.ascontiguousarray(gpad[b * 768:(b + 1) * 768]),
            my_u=np.ascontiguousarray(upad[b * 768:(b + 1) * 768]),
            my_d=np.ascontiguousarray(didx[b * 256:(b + 1) * 256]),
        ))
    return maps


def kernel(**inputs):
    in_maps = make_in_maps(**inputs)
    lut_vals = [float(v) for v in np.asarray(inputs["L0_ffn_lut"],
                                             np.float32).reshape(16)]
    nc = build_program(NT_FULL, lut_vals=lut_vals)
    mybir.codegen_inst_isa_subclasses(nc)
    res = run_bass_kernel_spmd(nc, in_maps, core_ids=list(range(8)))
    return np.stack([np.asarray(r["out"], dtype=np.float32)
                     for r in res.results], axis=0)


# version 24 (final)
# speedup vs baseline: 1.0139x; 1.0139x over previous
import os
import numpy as np
import ml_dtypes
import concourse.bass as bass
import concourse.mybir as mybir
from concourse.bass_utils import run_bass_kernel_spmd
from concourse import library_config

F32 = mybir.dt.float32
BF16 = mybir.dt.bfloat16
I32 = mybir.dt.int32
AF = mybir.ActivationFunctionType
ALU = mybir.AluOpType
AX = mybir.AxisListType

D = 2048
S_FULL = 4096
TT = 512
NT_FULL = S_FULL // TT
NKC = D // 128
NOC = 32
NIC = 43
HV = 16
HQ = 8
DKSC = float(128.0 ** -0.5)


class _Buf:
    __slots__ = ("last_w", "readers")

    def __init__(self):
        self.last_w = None
        self.readers = []


ENGS = ("PE", "V", "SC", "G", "SY")


class Prog:

    N_SY_LANES = 16
    N_G_LANES = 4

    def __init__(self):
        self.ops = {e: [] for e in ENGS}
        self.bufs = {}
        self.barrier_at = None
        self.lane_count = {"SY": 0, "G": 0}

    def buf(self, *key):
        b = self.bufs.get(key)
        if b is None:
            b = _Buf()
            self.bufs[key] = b
        return b

    def op(self, eng, emit, reads=(), writes=(), reads_nodep=(), dma=False,
           cc=False, tag=""):
        real = "G" if eng == "GD" else eng
        dma = dma or (eng == "GD")
        lane = None
        if cc:
            i = self.lane_count.setdefault("CC", 0)
            self.lane_count["CC"] = i + 1
            lane = ("CC", i, 0)
        elif dma:
            lane = (real, -1, -1)
        idx = len(self.ops[real])
        deps = set()
        soft = set()

        def add(e, i):
            if e == "PE" and real == "PE" and self.ops["PE"][i][3] is None:
                soft.add(i)
                return
            deps.add((e, i))
        for b in reads:
            if b.last_w is not None:
                add(*b.last_w)
        for b in writes:
            if b.last_w is not None:
                add(*b.last_w)
            for r in b.readers:
                add(*r)
        self.ops[real].append([emit, sorted(deps),
                               False, lane, tag, sorted(soft)])
        me = (real, idx)
        for b in reads:
            b.readers.append(me)
        for b in reads_nodep:
            b.readers.append(me)
        for b in writes:
            b.last_w = me
            b.readers = []

    def barrier(self):
        tails = []
        seen_lane = set()
        for e in ENGS:
            got_compute = False
            for i in range(len(self.ops[e]) - 1, -1, -1):
                o = self.ops[e][i]
                if o[3] is None:
                    if not got_compute:
                        tails.append((e, i))
                        got_compute = True
                else:
                    key = (o[3][0], o[3][1])
                    if key not in seen_lane:
                        seen_lane.add(key)
                        tails.append((e, i))

        self.barrier_at = ({e: len(self.ops[e]) for e in ENGS}, tails)

    EST = {
        "PE": {"qkmm": 310, "gmm": 310, "omm": 310, "fgmm": 310, "fumm": 310,
               "fdmm": 310, "bamm": 310, "ks0": 135, "qs0": 135, "akk": 135,
               "akq": 135, "ssq1": 135, "ssq2": 135, "sapi": 135, "sapg": 135,
               "oin": 135, "snmm": 135, "lntr": 150, "vtr": 150, "ktr": 150,
               "g1tr": 150, "ctr": 150, "otr": 150, "p0tr": 150, "dqtr": 150,
               "trL": 150, "ltr": 200, "rcm": 150, "l2mm": 300, "krep": 350,
               "p0ba": 150, "_d": 200},
        "V": {"lnred": 2300, "lnn": 1250, "cvev": 700, "cvtl": 80, "cvnt": 80,
              "cv3": 260, "cvk": 420, "l2sq": 420, "khat": 800, "sapev": 460,
              "gb2": 460, "mb2": 320, "rt": 460, "g1ev": 400, "osum": 460,
              "orsq": 360, "orred": 360, "gmul": 460, "sup": 460, "sbf": 320,
              "vtev": 620, "rkcv": 90, "rqcv": 90, "grq": 70, "dts": 320,
              "dqj": 1600, "dqa": 3200, "dqcv": 2600, "dq0": 3000,
              "lnev": 1300, "oadd": 1300, "oev": 700, "fdev": 1500,
              "fmul": 700, "uev": 700, "trLev": 110, "dbg": 1300, "dbm": 1300,
              "scanL": 400, "scanS": 400, "bala": 600, "_d": 450},
        "SC": {"akkev": 380, "akqev": 380, "ssqev": 540, "ub": 470,
               "kdec": 470, "onb": 470, "oea": 470, "orln": 340, "orex": 340, "orsq": 540,
               "dexp": 1700, "basig": 940, "baexp": 940, "gsl": 760,
               "cvsl": 760, "fsl": 760, "l2ln": 940, "l2ex": 940,
               "lnsq": 3400, "p0ev": 1700, "ktev": 380, "_d": 540},
        "G": {"lbc": 900, "_d": 2500},
        "SY": {"qkld": 2600, "gld": 2600, "old": 2600, "fgld": 2600,
               "fuld": 2600, "fdwl": 2200, "fdhl": 1700, "lnld": 3600,
               "xold": 3600, "ost": 3600, "dqld": 6000, "p0st": 2600,
               "fhst": 1700, "_d": 3000},
        "GD": {"p0ld": 4200, "_d": 4200},
        "CC": {"_d": 160000},
    }

    def _est(self, pool, tag):
        t = self.EST.get(pool, self.EST["SY"])
        return t.get(tag, t["_d"])

    def finalize(self, nc, sems, lane_sems):
        import heapq
        bar_marks_orig = (self.barrier_at[0] if self.barrier_at is not None
                          else {e: 0 for e in ENGS})
        nodes = []
        id_of = {}
        for e in ENGS:
            for i in range(len(self.ops[e])):
                id_of[(e, i)] = len(nodes)
                nodes.append((e, i))
        N = len(nodes)
        succ = [[] for _ in range(N)]
        indeg = [0] * N
        hard = [None] * N
        for e in ENGS:
            for i, o in enumerate(self.ops[e]):
                me = id_of[(e, i)]
                for (de, di) in o[1]:
                    succ[id_of[(de, di)]].append((me, True))
                    indeg[me] += 1
                for si in o[5]:
                    succ[id_of[("PE", si)]].append((me, False))
                    indeg[me] += 1
        pool_of = [None] * N
        epoch = [0] * N
        durs = [0] * N
        for e in ENGS:
            for i, o in enumerate(self.ops[e]):
                nid = id_of[(e, i)]
                if o[3] is None:
                    p = e
                elif o[3][0] == "CC":
                    p = "CC"
                elif e == "G":
                    p = "GD"
                else:
                    p = "SY"
                pool_of[nid] = p
                tag = o[4] if o[4] else "_d"
                durs[nid] = self._est(p if p in self.EST else e, tag)
                epoch[nid] = 0 if i < bar_marks_orig.get(e, 0) else 1
        if self.barrier_at is None:
            epoch = [0] * N
        cp = [0.0] * N
        order_topo = []
        indeg2 = indeg[:]
        stack = [i for i in range(N) if not succ[i]]
        nsucc = [len(succ[i]) for i in range(N)]
        rem = nsucc[:]
        stack = [i for i in range(N) if rem[i] == 0]
        preds = [[] for _ in range(N)]
        for i in range(N):
            for (m, _h) in succ[i]:
                preds[m].append(i)
        while stack:
            i = stack.pop()
            cp[i] += durs[i]
            for p2 in preds[i]:
                if cp[i] > cp[p2]:
                    cp[p2] = cp[i]
                rem[p2] -= 1
                if rem[p2] == 0:
                    stack.append(p2)

        HOP = 150
        n_units = {"PE": 1, "V": 1, "SC": 1, "G": 1, "SY": self.N_SY_LANES,
                   "GD": self.N_G_LANES, "CC": 3}
        start_t = [0.0] * N
        end_t = [0.0] * N
        lane_pick = [None] * N
        sched_order = []
        for ep in (0, 1):
            free = {p: [0.0] * n for p, n in n_units.items()}
            if ep == 1 and self.barrier_at is not None:
                tbar = max([0.0] + [end_t[i] for i in range(N) if epoch[i] == 0])
                free = {p: [tbar] * n for p, n in n_units.items()}
            ready = {p: [] for p in n_units}
            rt = [0.0] * N
            cnt = [0] * N
            for i in range(N):
                if epoch[i] != ep:
                    continue
                need = 0
                for e2 in ENGS:
                    pass
                cnt[i] = indeg[i]
            for e in ENGS:
                for i, o in enumerate(self.ops[e]):
                    nid = id_of[(e, i)]
                    if epoch[nid] != ep:
                        continue
                    c = 0
                    r = 0.0
                    for (de, di) in o[1]:
                        did = id_of[(de, di)]
                        if epoch[did] == ep:
                            c += 1
                        else:
                            r = max(r, end_t[did] + HOP)
                    for si in o[5]:
                        did = id_of[("PE", si)]
                        if epoch[did] == ep:
                            c += 1
                        else:
                            r = max(r, end_t[did])
                    cnt[nid] = c
                    rt[nid] = r
                    if c == 0:
                        heapq.heappush(ready[pool_of[nid]], (rt[nid], nid))
            n_done = 0
            n_total = sum(1 for i in range(N) if epoch[i] == ep)
            while n_done < n_total:
                best = None
                for p, h in ready.items():
                    if not h:
                        continue
                    r0, nid0 = h[0]
                    fmin = min(free[p])
                    stt = max(r0, fmin)
                    if best is None or stt < best[0]:
                        best = (stt, p, nid0)
                assert best is not None, "scheduler deadlock"
                stt, p, nid = best
                pool_h = ready[p]
                cand = []
                while pool_h and pool_h[0][0] <= stt + 60:
                    cand.append(heapq.heappop(pool_h))
                sel = max(range(len(cand)), key=lambda k: cp[cand[k][1]])
                nid = cand[sel][1]
                for k in range(len(cand)):
                    if k != sel:
                        heapq.heappush(pool_h, cand[k])
                li = min(range(n_units[p]), key=lambda k: free[p][k])
                stt = max(rt[nid], free[p][li])
                start_t[nid] = stt
                end_t[nid] = stt + durs[nid]
                free[p][li] = end_t[nid]
                lane_pick[nid] = li
                sched_order.append(nid)
                n_done += 1
                for (m, is_hard) in succ[nid]:
                    if epoch[m] != ep:
                        continue
                    rt[m] = max(rt[m], end_t[nid] + (HOP if is_hard else 0))
                    cnt[m] -= 1
                    if cnt[m] == 0:
                        heapq.heappush(ready[pool_of[m]], (rt[m], m))
        lane_seq = {}
        for nid in sorted(range(N), key=lambda i: (start_t[i], i)):
            e, i = nodes[nid]
            o = self.ops[e][i]
            if o[3] is None:
                continue
            if o[3][0] == "CC":
                continue
            q = "SY" if pool_of[nid] == "SY" else "G"
            key = (q, lane_pick[nid])
            s = lane_seq.get(key, 0)
            lane_seq[key] = s + 1
            o[3] = (q, lane_pick[nid], s)
        new_order = {e: [] for e in ENGS}
        for nid in sorted(range(N), key=lambda i: (start_t[i], i)):
            e, i = nodes[nid]
            new_order[e].append(i)
        for e in ENGS:
            for o in self.ops[e]:
                for (de, di) in o[1]:
                    self.ops[de][di][2] = True
        val = {}
        for e in ENGS:
            cc = 0
            for i in new_order[e]:
                o = self.ops[e][i]
                if o[3] is not None:
                    o[2] = True
                    step = 1 if o[3][0] == "CC" else 16
                    val[(e, i)] = ("L", o[3][0], o[3][1], step * (o[3][2] + 1))
                else:
                    if o[2]:
                        cc += 1
                    val[(e, i)] = ("C", e, cc)
        eng_map = {"PE": nc.tensor, "V": nc.vector, "SC": nc.scalar,
                   "G": nc.gpsimd, "SY": nc.sync}

        def sem_of(v):
            if v[0] == "C":
                return sems[v[1]], v[2], ("C", v[1])
            return lane_sems[(v[1], v[2])], v[3], ("L", v[1], v[2])

        bar_tails = self.barrier_at[1] if self.barrier_at is not None else []
        for e in ENGS:
            engine = eng_map[e]
            sat = {}
            bar_fired = self.barrier_at is None
            for i in new_order[e]:
                o = self.ops[e][i]
                deps = list(o[1])
                if not bar_fired and epoch[id_of[(e, i)]] == 1:
                    bar_fired = True
                    for (te, tidx) in bar_tails:
                        to = self.ops[te][tidx]
                        if to[3] is not None or te != e:
                            deps.append((te, tidx))
                want = {}
                for (de, di) in deps:
                    sem, tv, key = sem_of(val[(de, di)])
                    if tv > want.get(key, (0, None))[0]:
                        want[key] = (tv, sem)
                for key, (tv, sem) in want.items():
                    if tv <= sat.get(key, 0):
                        continue
                    engine.wait_ge(sem, tv)
                    sat[key] = tv
                if o[3] is not None and o[3][0] != "CC" and o[3][2] > 0:
                    key = ("L", o[3][0], o[3][1])
                    tv = 16 * o[3][2]
                    if tv > sat.get(key, 0):
                        engine.wait_ge(lane_sems[(o[3][0], o[3][1])], tv)
                        sat[key] = tv
                inst = o[0]()
                if o[2] and inst is not None:
                    if o[3] is not None:
                        inst.then_inc(lane_sems[(o[3][0], o[3][1])],
                                      1 if o[3][0] == "CC" else 16)
                    else:
                        inst.then_inc(sems[e], 1)
        engine = eng_map["SY"]
        lane_tot = {}
        for e in ENGS:
            ncnt = 0
            for o in self.ops[e]:
                if o[3] is not None:
                    key = (o[3][0], o[3][1])
                    step = 1 if o[3][0] == "CC" else 16
                    lane_tot[key] = max(lane_tot.get(key, 0),
                                        step * (o[3][2] + 1))
                elif o[2]:
                    ncnt += 1
            if ncnt:
                engine.wait_ge(sems[e], ncnt)
        for key, tv in lane_tot.items():
            engine.wait_ge(lane_sems[key], tv)
        self.sched_makespan = max(end_t) if N else 0.0
        if os.environ.get("KDIAG"):
            import collections
            busy = collections.Counter()
            tagsum = collections.Counter()
            for nid in range(N):
                busy[pool_of[nid]] += durs[nid]
                e, i = nodes[nid]
                tagsum[(pool_of[nid], self.ops[e][i][4])] += durs[nid]
            ms = self.sched_makespan
            print(f"[sched] makespan {ms/1e6:.2f}ms")
            for p, b in busy.most_common():
                print(f"[sched] {p:3s} busy {b/1e6:7.2f}ms  util {b/ms*100:4.0f}%")
            for (p, t), b in tagsum.most_common(24):
                print(f"[sched]   {p:3s} {t:6s} {b/1e6:7.2f}ms")
            pred_best = [None] * N
            for nid in range(N):
                e, i = nodes[nid]
                o = self.ops[e][i]
                for (de, di) in o[1]:
                    pred_best[nid] = pred_best[nid]
            last = max(range(N), key=lambda i: end_t[i])
            path = []
            cur = last
            for _ in range(100000):
                e, i = nodes[cur]
                o = self.ops[e][i]
                cands = [id_of[d] for d in o[1]] + [id_of[("PE", s)] for s in o[5]]
                path.append(cur)
                best = None
                for d in cands:
                    if best is None or end_t[d] > end_t[best]:
                        best = d
                if best is None or end_t[best] + 1500 < start_t[cur]:
                    break
                cur = best
            cp = collections.Counter()
            for nid in path:
                e, i = nodes[nid]
                cp[(pool_of[nid], self.ops[e][i][4])] += durs[nid]
            print(f"[sched] cp-walk len {len(path)} dur {sum(durs[n] for n in path)/1e6:.2f}ms (gaps excl)")
            for (p, t), b in cp.most_common(14):
                print(f"[sched]   CP {p:3s} {t:6s} {b/1e6:6.2f}ms")


def build_program(n_tiles=NT_FULL, lut_vals=None):
    if lut_vals is None:
        lut_vals = [0.0] * 16
    from contextlib import ExitStack
    nc = bass.Bass("TRN2", target_bir_lowering=False, debug=False)

    x_d = nc.dram_tensor("x", [S_FULL, D], F32, kind="ExternalInput").ap()
    wqkv_d = nc.dram_tensor("w_qkv", [4096, D], F32, kind="ExternalInput").ap()
    wg_d = nc.dram_tensor("w_gate", [D, D], F32, kind="ExternalInput").ap()
    wo_d = nc.dram_tensor("w_out", [D, D], F32, kind="ExternalInput").ap()
    wba_d = nc.dram_tensor("w_ba", [32, D], F32, kind="ExternalInput").ap()
    convw_d = nc.dram_tensor("conv_w", [4096, 4], F32, kind="ExternalInput").ap()
    logA_d = nc.dram_tensor("log_A", [16, 1], F32, kind="ExternalInput").ap()
    dtb_d = nc.dram_tensor("dt_bias", [16, 1], F32, kind="ExternalInput").ap()
    rmsw_d = nc.dram_tensor("rms_w", [128, 1], F32, kind="ExternalInput").ap()
    myg_d = nc.dram_tensor("my_g", [6 * 128, D], I32, kind="ExternalInput").ap()
    myu_d = nc.dram_tensor("my_u", [6 * 128, D], I32, kind="ExternalInput").ap()
    myd_d = nc.dram_tensor("my_d", [2 * 128, 5504], I32, kind="ExternalInput").ap()
    cidb_d = nc.dram_tensor("cid_bf", [128, 128], BF16, kind="ExternalInput").ap()
    cidf_d = nc.dram_tensor("cid_f", [128, 128], F32, kind="ExternalInput").ap()
    cincl_d = nc.dram_tensor("cm_incl", [128, 128], BF16, kind="ExternalInput").ap()
    csneg_d = nc.dram_tensor("cm_sneg", [128, 128], BF16, kind="ExternalInput").ap()
    cincl8_d = nc.dram_tensor("cm_incl8", [128, 1024], BF16, kind="ExternalInput").ap()
    csneg8_d = nc.dram_tensor("cm_sneg8", [128, 1024], BF16, kind="ExternalInput").ap()
    out_d = nc.dram_tensor("out", [S_FULL, D], F32, kind="ExternalOutput").ap()

    s_wq = nc.dram_tensor("s_wq", [128, NOC, NKC, 128], BF16, kind="Internal").ap()
    s_wg = nc.dram_tensor("s_wg", [128, NKC, NKC, 128], BF16, kind="Internal").ap()
    s_wo = nc.dram_tensor("s_wo", [128, NKC, NKC, 128], BF16, kind="Internal").ap()
    ag_g_in = nc.dram_tensor("ag_g_in", [6, 128, NKC * 128], BF16, kind="Internal").ap()
    ag_u_in = nc.dram_tensor("ag_u_in", [6, 128, NKC * 128], BF16, kind="Internal").ap()
    ag_d_in = nc.dram_tensor("ag_d_in", [2, 128, NIC * 128], BF16, kind="Internal").ap()
    ag_g = nc.dram_tensor("ag_g", [48, 128, NKC * 128], BF16, kind="Internal").ap()
    ag_u = nc.dram_tensor("ag_u", [48, 128, NKC * 128], BF16, kind="Internal").ap()
    ag_d = nc.dram_tensor("ag_d", [16, 128, NIC * 128], BF16, kind="Internal").ap()
    s_hact = nc.dram_tensor("s_hact", [NIC, 128, TT], BF16, kind="Internal").ap()

    P = Prog()
    bm = P.buf
    st = ExitStack()
    sb = lambda name, shape, dt: st.enter_context(nc.sbuf_tensor(name, shape, dt))
    ps = lambda name, shape, dt: st.enter_context(nc.psum_tensor(name, shape, dt))

    sems = {e: st.enter_context(nc.semaphore(f"sem_{e}")) for e in ENGS}
    lane_sems = {}
    for i in range(Prog.N_SY_LANES):
        lane_sems[("SY", i)] = st.enter_context(nc.semaphore(f"dsy{i}"))
    for i in range(Prog.N_G_LANES):
        lane_sems[("G", i)] = st.enter_context(nc.semaphore(f"dg{i}"))
    for i in range(3):
        lane_sems[("CC", i)] = st.enter_context(nc.semaphore(f"cc{i}"))

    m_idb = sb("m_idb", [128, 128], BF16)
    m_idf = sb("m_idf", [128, 128], F32)
    m_incl = sb("m_incl", [128, 128], BF16)
    m_sneg = sb("m_sneg", [128, 128], BF16)
    m_incl8 = sb("m_incl8", [128, 1024], BF16)
    m_sneg8 = sb("m_sneg8", [128, 1024], BF16)
    ones_rb = sb("ones_rb", [1, 128], BF16)
    ones_cb = sb("ones_cb", [128, 1], BF16)
    one_b = sb("one_b", [1, 1], BF16)
    wbaT = sb("wbaT", [128, NKC, 64], BF16)
    convw = sb("convw", [128, NOC, 4], F32)
    rmsw = sb("rmsw", [128, 1], F32)
    logA_c = sb("logA_c", [16, 1], F32)
    negA = sb("negA", [16, 1], F32)
    dtb = sb("dtb", [16, 1], F32)
    eps5 = sb("eps5", [128, 1], F32)
    eps6 = sb("eps6", [128, 1], F32)

    B = [ps(f"pb{i}", [128, 512], F32) for i in range(6)]
    BT = [ps("pbt0", [128, 1024], BF16), ps("pbt1", [128, 1024], BF16)]

    P.op("G", lambda: nc.gpsimd.load_library(library_config.proxy))
    P.op("SY", lambda: nc.sync.dma_start(m_idb[:], cidb_d[:]),
         writes=[bm("m_idb")], dma=True)
    P.op("SY", lambda: nc.sync.dma_start(m_idf[:], cidf_d[:]),
         writes=[bm("m_idf")], dma=True)
    P.op("SY", lambda: nc.sync.dma_start(m_incl[:], cincl_d[:]),
         writes=[bm("m_incl")], dma=True)
    P.op("SY", lambda: nc.sync.dma_start(m_sneg[:], csneg_d[:]),
         writes=[bm("m_sneg")], dma=True)
    P.op("SY", lambda: nc.sync.dma_start(m_incl8[:], cincl8_d[:]),
         writes=[bm("m_incl8")], dma=True)
    P.op("SY", lambda: nc.sync.dma_start(m_sneg8[:], csneg8_d[:]),
         writes=[bm("m_sneg8")], dma=True)
    P.op("SY", lambda: nc.sync.dma_start(
        convw[:, :, :], convw_d.rearrange("(a p) k -> p a k", p=128)),
         writes=[bm("convw")], dma=True)
    P.op("SY", lambda: nc.sync.dma_start(rmsw[:], rmsw_d[:]),
         writes=[bm("rmsw")], dma=True)
    P.op("SY", lambda: nc.sync.dma_start(logA_c[:], logA_d[:]),
         writes=[bm("logA")], dma=True)
    P.op("SY", lambda: nc.sync.dma_start(dtb[:], dtb_d[:]),
         writes=[bm("dtb")], dma=True)
    P.op("V", lambda: nc.vector.memset(ones_rb[:1, :], 1.0), writes=[bm("ones_rb")])
    P.op("V", lambda: nc.vector.memset(ones_cb[:], 1.0), writes=[bm("ones_cb")])
    P.op("V", lambda: nc.vector.memset(one_b[:1, :], 1.0), writes=[bm("one_b")])
    P.op("V", lambda: nc.vector.memset(eps5[:], 1e-5), writes=[bm("eps5")])
    P.op("V", lambda: nc.vector.memset(eps6[:], 1e-6), writes=[bm("eps6")])
    P.op("SC", lambda: nc.scalar.activation(negA[:], logA_c[:], AF.Exp),
         reads=[bm("logA")], writes=[bm("negA")])
    P.op("V", lambda: nc.vector.tensor_scalar(negA[:], negA[:], -1.0, None, ALU.mult),
         reads=[bm("negA")], writes=[bm("negA")])


    st0 = ExitStack()
    sb0 = lambda name, shape, dt: st0.enter_context(nc.sbuf_tensor(name, shape, dt))
    wcast = [sb0(f"p0_wcast{i}", [128, D], BF16) for i in range(2)]
    q_idx = sb0("p0_idx", [128, 5504], I32)
    q_idxb = sb0("p0_idxb", [128, 5504], BF16)
    q_w = sb0("p0_w", [128, 5504], BF16)
    q_t = sb0("p0_t", [128, 5504], BF16)
    q_g = sb0("p0_g", [128, 5504], BF16)
    dstage = [sb0(f"p0_dst{i}", [128, NIC * 128], BF16) for i in range(2)]
    wba_f = sb0("p0_wba", [32, D], F32)
    wba_b = sb0("p0_wbab", [32, D], BF16)

    def transpose_store(src_sb, src_buf, stage_sb, stage_buf, dst_ap, dst_buf, nkc):
        for half in range((nkc + 7) // 8):
            k0 = half * 8
            k1 = min(k0 + 8, nkc)
            bi = half % 2
            for kc in range(k0, k1):
                P.op("PE", (lambda kc=kc, k0=k0, bi=bi, s=src_sb:
                            nc.tensor.transpose(
                                BT[bi][:, (kc - k0) * 128:(kc - k0 + 1) * 128],
                                s[:, kc * 128:(kc + 1) * 128], m_idb[:])),
                     reads=[src_buf, bm("m_idb")], writes=[bm("BT", bi)], tag="p0tr")
            P.op("SC", (lambda k0=k0, k1=k1, bi=bi, s=stage_sb:
                        nc.scalar.copy(s[:, k0 * 128:k1 * 128],
                                       BT[bi][:, 0:(k1 - k0) * 128])),
                 reads=[bm("BT", bi)], writes=[stage_buf], tag="p0ev")
        P.op("SY", (lambda s=stage_sb, d=dst_ap, nk=nkc:
                    nc.sync.dma_start(d, s[:, 0:nk * 128].rearrange(
                        "p (a b) -> p a b", b=128))),
             reads=[stage_buf], writes=[dst_buf], dma=True, tag="p0st")

    for (w_ap, scr, n_oc, sn) in ((wqkv_d, s_wq, NOC, "s_wq"),
                                  (wg_d, s_wg, NKC, "s_wg"),
                                  (wo_d, s_wo, NKC, "s_wo")):
        for oc in range(n_oc):
            sl = oc % 2
            P.op("GD", (lambda w=w_ap, oc=oc, t=wcast[sl]:
                        nc.gpsimd.dma_start(t[:], w[oc * 128:(oc + 1) * 128, :])),
                 writes=[bm("wcast", sl)], tag="p0ld")
            transpose_store(wcast[sl], bm("wcast", sl), dstage[sl],
                            bm("dstage", sl), scr[:, oc, :, :], bm(sn, oc), NKC)

    P.op("V", lambda: nc.vector.memset(wbaT[:].rearrange("p a b -> p (a b)"), 0.0),
         writes=[bm("wbaT")])
    P.op("SY", lambda: nc.sync.dma_start(wba_f[:], wba_d[:]),
         writes=[bm("wba_f")], dma=True)
    P.op("V", lambda: nc.vector.tensor_copy(wba_b[:], wba_f[:]),
         reads=[bm("wba_f")], writes=[bm("wba_b")])
    for kc in range(NKC):
        P.op("PE", (lambda kc=kc: nc.tensor.transpose(
            BT[0][:, 0:32], wba_b[0:32, kc * 128:(kc + 1) * 128],
            m_idb[0:32, 0:32])),
             reads=[bm("wba_b"), bm("m_idb")], writes=[bm("BT", 0)])
        P.op("V", (lambda kc=kc: nc.vector.tensor_copy(wbaT[:, kc, 0:16],
                                                       BT[0][:, 0:16])),
             reads=[bm("BT", 0)], writes=[bm("wbaT")])
        P.op("V", (lambda kc=kc: nc.vector.tensor_copy(wbaT[:, kc, 32:48],
                                                       BT[0][:, 16:32])),
             reads=[bm("BT", 0)], writes=[bm("wbaT")])

    for (idx_ap, agin, n_rb, n_cols, inname) in (
            (myg_d, ag_g_in, 6, D, "ag_g_in"),
            (myu_d, ag_u_in, 6, D, "ag_u_in"),
            (myd_d, ag_d_in, 2, 5504, "ag_d_in")):
        nkc = n_cols // 128
        for rb in range(n_rb):
            P.op("SY", (lambda a=idx_ap, rb=rb, n=n_cols:
                        nc.sync.dma_start(q_idx[:, 0:n], a[rb * 128:(rb + 1) * 128, :])),
                 writes=[bm("q_idx")], dma=True, tag="dqld")
            P.op("V", (lambda n=n_cols: nc.vector.tensor_copy(q_idxb[:, 0:n],
                                                              q_idx[:, 0:n])),
                 reads=[bm("q_idx")], writes=[bm("q_idxb")], tag="dqcv")
            P.op("V", (lambda n=n_cols: nc.vector.tensor_scalar(
                q_w[:, 0:n], q_idxb[:, 0:n], 0.0, float(lut_vals[0]),
                ALU.is_equal, ALU.mult)),
                 reads=[bm("q_idxb")], writes=[bm("q_w")], tag="dq0")
            for j in range(1, 16):
                P.op("V", (lambda n=n_cols, j=j: nc.vector.tensor_single_scalar(
                    q_t[:, 0:n], q_idxb[:, 0:n], float(j), ALU.is_equal)),
                     reads=[bm("q_idxb")], writes=[bm("q_t")], tag="dqj")
                a_src, a_dst = (q_w, q_g) if j % 2 == 1 else (q_g, q_w)
                b_src = bm("q_w") if j % 2 == 1 else bm("q_g")
                b_dst = bm("q_g") if j % 2 == 1 else bm("q_w")
                P.op("V", (lambda n=n_cols, j=j, a=a_src, d=a_dst:
                           nc.vector.scalar_tensor_tensor(
                    d[:, 0:n], q_t[:, 0:n], float(lut_vals[j]), a[:, 0:n],
                    ALU.mult, ALU.add)),
                     reads=[bm("q_t"), b_src], writes=[b_dst], tag="dqa")
            transpose_store(q_g, bm("q_g"), dstage[rb % 2], bm("dstage", rb % 2),
                            agin[rb, :, :].rearrange("p (a b) -> p a b", b=128),
                            bm(inname, rb), nkc)
    P.op("G", lambda: nc.gpsimd.collective_compute(
        "AllGather", ALU.bypass, replica_groups=[list(range(8))],
        ins=[ag_g_in[:, :, :].opt()], outs=[ag_g[:, :, :].opt()]),
         reads=[bm("ag_g_in", b) for b in range(6)],
         writes=[bm("s_fg", oc) for oc in range(48)] + [bm("cc_ord")],
         cc=True, tag="agg")
    P.op("G", lambda: nc.gpsimd.collective_compute(
        "AllGather", ALU.bypass, replica_groups=[list(range(8))],
        ins=[ag_u_in[:, :, :].opt()], outs=[ag_u[:, :, :].opt()]),
         reads=[bm("ag_u_in", b) for b in range(6)],
         writes=[bm("s_fu", oc) for oc in range(48)] + [bm("cc_ord")],
         cc=True, tag="agu")
    P.op("G", lambda: nc.gpsimd.collective_compute(
        "AllGather", ALU.bypass, replica_groups=[list(range(8))],
        ins=[ag_d_in[:, :, :].opt()], outs=[ag_d[:, :, :].opt()]),
         reads=[bm("ag_d_in", b) for b in range(2)],
         writes=[bm("s_fd", hc) for hc in range(16)] + [bm("cc_ord")],
         cc=True, tag="agd")

    st0.close()
    P.barrier()

    xnT = sb("xnT", [128, NKC, TT], BF16)
    cooked = sb("cooked", [128, NOC, TT], BF16)
    gateb = sb("gateb", [128, NKC, TT], BF16)
    accb = sb("accb", [128, NKC, TT], BF16)
    hstage = [sb(f"hstage{i}", [128, TT], BF16) for i in range(2)]
    ub_t = [sb(f"ubt{i}", [128, TT], BF16) for i in range(2)]
    hring = [sb(f"hring{i}", [128, TT], BF16) for i in range(4)]
    S32 = sb("S32", [128, HV, 128], F32)
    Sbf = sb("Sbf", [128, HV, 128], BF16)
    tails = sb("tails", [128, NOC, 3], F32)
    raw_ext = sb("raw_ext", [128, 515], BF16)
    cvt_b = sb("cvt_b", [128, 512], BF16)
    sg_b = sb("sg_b", [128, 512], BF16)
    xst = sb("xst", [128, D], F32)
    xst2 = sb("xst2", [128, D], F32)
    xout = sb("xout", [128, D], F32)
    xntok = sb("xntok", [128, D], BF16)
    mcol = sb("mcol", [128, 1], F32)
    s2col = sb("s2col", [128, 1], F32)
    vtmp1 = sb("vtmp1", [128, 1], F32)
    rstd = sb("rstd", [128, 1], F32)
    rrow_f = sb("rrow_f", [1, TT], F32)
    rrow_b = sb("rrow_b", [1, TT], BF16)
    rkcols = sb("rkcols", [128, HQ, 4], F32)
    rqcols = sb("rqcols", [128, HQ, 4], F32)
    la_f = sb("la_f", [16, TT], F32)
    suf_f = sb("suf_f", [16, TT], F32)
    beta_f = sb("beta_f", [16, TT], F32)
    sp_t = sb("sp_t", [16, TT], F32)
    ones16 = sb("ones16", [16, 128], F32)
    Lt = sb("Lt", [128, 4, 16], F32)
    sufT = sb("sufT", [128, 4, 16], F32)
    laT = sb("laT", [128, 4, 16], F32)
    betaT = sb("betaT", [128, 4, 16], F32)
    gam = sb("gam", [128, 4, 16], F32)
    gamneg = sb("gamneg", [128, 4, 16], F32)
    grep = sb("grep", [128, 4, 16], F32)
    eKD = sb("eKD", [128, 4, 16], F32)
    tcol16 = sb("tcol16", [128, 16], F32)
    Dm = sb("Dm", [128, 4, 128], F32)
    DbG = sb("DbG", [128, HV, 128], BF16)
    DbM = sb("DbM", [128, HV, 128], BF16)
    grq = sb("grq", [128, 16], F32)
    lrow = sb("lrow", [1, 128], F32)
    khatT = sb("khatT", [128, HQ, 128], BF16)
    Akk = sb("Akk", [128, HQ, 128], BF16)
    Akq = sb("Akq", [128, HQ, 128], BF16)
    Vt_all = sb("Vt_all", [128, HV, 128], BF16)
    NSL = 2
    Gbuf = [[sb(f"Gc{s}", [128, 256], BF16), sb(f"Gn{s}", [128, 256], BF16)]
            for s in range(NSL)]
    xch = [sb(f"xch{s}", [128, 2, 128], BF16) for s in range(NSL)]
    Mp = [sb(f"Mp{s}", [128, 128], BF16) for s in range(NSL)]
    Rt = [sb(f"Rt{s}", [128, 128], BF16) for s in range(NSL)]
    Ub = [sb(f"Ub{s}", [128, 128], BF16) for s in range(NSL)]
    Kdec = [sb(f"Kdec{s}", [128, 128], BF16) for s in range(NSL)]
    g1t = [sb(f"g1t{s}", [128, 128], BF16) for s in range(NSL)]
    ea_f = [sb(f"ea{s}", [128, 128], F32) for s in range(NSL)]
    O_f = [sb(f"Of{s}", [128, 128], F32) for s in range(NSL)]
    osq = [sb(f"osq{s}", [128, 128], F32) for s in range(NSL)]
    ossq = [sb(f"ossq{s}", [128, 1], F32) for s in range(NSL)]
    orf = [sb(f"orf{s}", [128, 1], F32) for s in range(NSL)]
    On_b = [sb(f"Onb{s}", [128, 128], BF16) for s in range(NSL)]
    wpool = sb("wpool", [128, 12288], BF16)

    def wslot16(i):
        return wpool[:, i * 2048:(i + 1) * 2048].rearrange(
            "p (a b) -> p a b", b=128)

    def wslot43(i):
        return wpool[:, i * 6144:i * 6144 + NIC * 128].rearrange(
            "p (a b) -> p a b", b=128)

    def wgran16(i):
        return [bm("wp", i)]

    def wgran43(i):
        return [bm("wp", 3 * i), bm("wp", 3 * i + 1), bm("wp", 3 * i + 2)]

    P.op("V", lambda: nc.vector.memset(S32[:].rearrange("p a b -> p (a b)"), 0.0),
         writes=[bm("S32", h) for h in range(HV)])
    P.op("V", lambda: nc.vector.memset(Sbf[:].rearrange("p a b -> p (a b)"), 0.0),
         writes=[bm("Sbf", h) for h in range(HV)])
    P.op("V", lambda: nc.vector.memset(tails[:].rearrange("p a b -> p (a b)"), 0.0),
         writes=[bm("tails", oc) for oc in range(NOC)])
    P.op("V", lambda: nc.vector.memset(ones16[0:16, :], 1.0), writes=[bm("ones16")])

    def stream16(scr_ap, scr_buf, sl, tag):
        P.op("SY", (lambda a=scr_ap, sl=sl: nc.sync.dma_start(
            wslot16(sl), a)),
             reads=[scr_buf], writes=wgran16(sl), dma=True, tag=tag)

    for ti in range(n_tiles):
        t0 = ti * TT

        for tc in range(4):
            r0 = t0 + tc * 128
            lnb = xst if tc % 2 == 0 else xst2
            lk = "xst" if tc % 2 == 0 else "xst2"
            P.op("SY", (lambda r0=r0, lnb=lnb: nc.sync.dma_start(
                lnb[:], x_d[r0:r0 + 128, :])),
                 writes=[bm(lk)], dma=True, tag="lnld")
            P.op("V", (lambda lnb=lnb: nc.vector.reduce_sum(
                mcol[:], lnb[:], axis=AX.X)),
                 reads=[bm(lk)], writes=[bm("mcol")], tag="lnred")
            P.op("V", lambda: nc.vector.tensor_scalar(mcol[:], mcol[:], 1.0 / D,
                                                      None, ALU.mult),
                 reads=[bm("mcol")], writes=[bm("mcol")])
            P.op("SC", (lambda lnb=lnb: nc.scalar.activation(
                xntok[:], lnb[:], AF.Square, accum_out=s2col[:])),
                 reads=[bm(lk)], writes=[bm("xntok"), bm("s2col")], tag="lnsq")
            P.op("V", lambda: nc.vector.tensor_mul(vtmp1[:], mcol[:], mcol[:]),
                 reads=[bm("mcol")], writes=[bm("vt1")])
            P.op("V", lambda: nc.vector.tensor_scalar(s2col[:], s2col[:], 1.0 / D,
                                                      None, ALU.mult),
                 reads=[bm("s2col")], writes=[bm("s2col")])
            P.op("V", lambda: nc.vector.tensor_sub(s2col[:], s2col[:], vtmp1[:]),
                 reads=[bm("s2col"), bm("vt1")], writes=[bm("s2col")])
            P.op("SC", lambda: nc.scalar.activation(rstd[:], s2col[:], AF.Ln,
                                                    bias=eps5[:]),
                 reads=[bm("s2col"), bm("eps5")], writes=[bm("rstd")])
            P.op("SC", lambda: nc.scalar.activation(rstd[:], rstd[:], AF.Exp,
                                                    scale=-0.5),
                 reads=[bm("rstd")], writes=[bm("rstd")])
            P.op("V", (lambda lnb=lnb: nc.vector.tensor_scalar(
                xntok[:], lnb[:], mcol[:], rstd[:], ALU.subtract, ALU.mult)),
                 reads=[bm(lk), bm("mcol"), bm("rstd")], writes=[bm("xntok")],
                 tag="lnn")
            for half in range(2):
                k0 = half * 8
                for kc in range(k0, k0 + 8):
                    P.op("PE", (lambda kc=kc, k0=k0, bi=half:
                                nc.tensor.transpose(
                                    BT[bi][:, (kc - k0) * 128:(kc - k0 + 1) * 128],
                                    xntok[:, kc * 128:(kc + 1) * 128], m_idb[:])),
                         reads=[bm("xntok"), bm("m_idb")], writes=[bm("BT", half)],
                         tag="lntr")
                P.op("V", (lambda k0=k0, bi=half, tc=tc: nc.vector.tensor_copy(
                    xnT[:, k0:k0 + 8, tc * 128:(tc + 1) * 128],
                    BT[bi][:].rearrange("p (a b) -> p a b", a=8))),
                     reads=[bm("BT", half)],
                     writes=[bm("xnT", k) for k in range(k0, k0 + 8)], tag="lnev")

        for kc in range(NKC):
            rds = ([bm("wbaT")] + [bm("xnT", k2) for k2 in range(NKC)]
                   if kc == 0 else [])
            P.op("PE", (lambda kc=kc: nc.tensor.matmul(
                B[5][0:64, :], wbaT[:, kc, :], xnT[:, kc, :],
                start=(kc == 0), stop=(kc == NKC - 1))),
                 reads=rds,
                 reads_nodep=([bm("wbaT"), bm("xnT", kc)] if kc else []),
                 writes=[bm("B", 5)], tag="bamm")
        P.op("SC", lambda: nc.scalar.activation(beta_f[0:16, :], B[5][0:16, :],
                                                AF.Exp, scale=-1.0),
             reads=[bm("B", 5)], writes=[bm("beta_f")], tag="basig")
        P.op("V", lambda: nc.vector.tensor_scalar(beta_f[0:16, :], beta_f[0:16, :],
                                                  1.0, None, ALU.add),
             reads=[bm("beta_f")], writes=[bm("beta_f")])
        P.op("V", lambda: nc.vector.reciprocal(beta_f[0:16, :], beta_f[0:16, :]),
             reads=[bm("beta_f")], writes=[bm("beta_f")])
        P.op("SC", lambda: nc.scalar.activation(sp_t[0:16, :], B[5][32:48, :],
                                                AF.Exp, bias=dtb[:]),
             reads=[bm("B", 5), bm("dtb")], writes=[bm("sp_t")], tag="baexp")
        P.op("SC", lambda: nc.scalar.activation(sp_t[0:16, :], sp_t[0:16, :],
                                                AF.Ln, bias=1.0),
             reads=[bm("sp_t")], writes=[bm("sp_t")])
        P.op("V", lambda: nc.vector.tensor_scalar(la_f[0:16, :], sp_t[0:16, :],
                                                  negA[:], None, ALU.mult),
             reads=[bm("sp_t"), bm("negA")], writes=[bm("la_f")], tag="bala")

        for c in range(4):
            cs = slice(c * 128, (c + 1) * 128)
            P.op("V", (lambda cs=cs: nc.vector.tensor_tensor_scan(
                sp_t[0:16, cs], ones16[0:16, :], la_f[0:16, cs], 0.0,
                ALU.mult, ALU.add)),
                 reads=[bm("la_f"), bm("ones16")], writes=[bm("sp_t")], tag="scanL")
            rev = slice((c + 1) * 128 - 1, (c * 128 - 1) if c > 0 else None, -1)
            P.op("V", (lambda rev=rev: nc.vector.tensor_tensor_scan(
                suf_f[0:16, rev], ones16[0:16, :], la_f[0:16, rev], 0.0,
                ALU.mult, ALU.add)),
                 reads=[bm("la_f"), bm("ones16")], writes=[bm("suf_f")], tag="scanS")
            for (src_t, skey, dst, dkey) in ((sp_t, "sp_t", Lt, "Lt"),
                                             (suf_f, "suf_f", sufT, "sufT"),
                                             (la_f, "la_f", laT, "laT"),
                                             (beta_f, "beta_f", betaT, "betaT")):
                P.op("PE", (lambda s=src_t, cs=cs: nc.tensor.transpose(
                    B[4][:, 0:16], s[0:16, cs], m_idf[0:16, 0:16])),
                     reads=[bm(skey), bm("m_idf")], writes=[bm("B", 4)], tag="trL")
                P.op("V", (lambda d=dst, c=c: nc.vector.tensor_copy(d[:, c, :],
                                                                    B[4][:, 0:16])),
                     reads=[bm("B", 4)], writes=[bm(dkey)], tag="trLev")
            P.op("SC", (lambda c=c: nc.scalar.activation(gam[:, c, :], Lt[:, c, :],
                                                         AF.Exp)),
                 reads=[bm("Lt")], writes=[bm("gam")])
            P.op("V", (lambda c=c: nc.vector.tensor_scalar(
                gamneg[:, c, :], gam[:, c, :], -1.0, None, ALU.mult)),
                 reads=[bm("gam")], writes=[bm("gamneg")])
            P.op("V", (lambda c=c: nc.vector.tensor_sub(tcol16[:], sufT[:, c, :],
                                                        laT[:, c, :])),
                 reads=[bm("sufT"), bm("laT")], writes=[bm("tcol16")])
            P.op("SC", (lambda c=c: nc.scalar.activation(eKD[:, c, :], tcol16[:],
                                                         AF.Exp)),
                 reads=[bm("tcol16")], writes=[bm("eKD")])
            P.op("V", (lambda c=c: nc.vector.tensor_add(tcol16[:], tcol16[:],
                                                        Lt[:, c, :])),
                 reads=[bm("tcol16"), bm("Lt")], writes=[bm("tcol16")])
            P.op("SC", (lambda c=c: nc.scalar.activation(grep[:, c, :], tcol16[:],
                                                         AF.Exp)),
                 reads=[bm("tcol16")], writes=[bm("grep")])

        for oc in range(NOC):
            sl = oc % 2
            stream16(s_wq[:, oc, :, :], bm("s_wq", oc), sl, "qkld")
            pi = oc % 2
            for kc in range(NKC):
                rds = (wgran16(sl) + [bm("xnT", k2) for k2 in range(NKC)]
                       if kc == 0 else [])
                P.op("PE", (lambda kc=kc, sl=sl, pi=pi: nc.tensor.matmul(
                    B[pi][:, :], wslot16(sl)[:, kc, :], xnT[:, kc, :],
                    start=(kc == 0), stop=(kc == NKC - 1))),
                     reads=rds, reads_nodep=(wgran16(sl) + [bm("xnT", kc)] if kc else []),
                     writes=[bm("B", pi)], tag="qkmm")
            P.op("V", (lambda oc=oc: nc.vector.tensor_copy(raw_ext[:, 0:3],
                                                           tails[:, oc, :])),
                 reads=[bm("tails", oc)], writes=[bm("raw_ext")], tag="cvtl")
            P.op("V", (lambda pi=pi: nc.vector.tensor_copy(raw_ext[:, 3:515],
                                                           B[pi][:, :])),
                 reads=[bm("B", pi)], writes=[bm("raw_ext")], tag="cvev")
            P.op("V", (lambda oc=oc: nc.vector.tensor_copy(tails[:, oc, :],
                                                           raw_ext[:, 512:515])),
                 reads=[bm("raw_ext")], writes=[bm("tails", oc)], tag="cvnt")
            P.op("V", (lambda oc=oc: nc.vector.tensor_scalar(
                cvt_b[:], raw_ext[:, 3:515], convw[:, oc, 3:4], None, ALU.mult)),
                 reads=[bm("raw_ext"), bm("convw")], writes=[bm("cvt")], tag="cv3")
            for k in (2, 1, 0):
                P.op("V", (lambda oc=oc, k=k: nc.vector.scalar_tensor_tensor(
                    cvt_b[:], raw_ext[:, k:k + 512], convw[:, oc, k:k + 1],
                    cvt_b[:], ALU.mult, ALU.add)),
                     reads=[bm("raw_ext"), bm("convw"), bm("cvt")],
                     writes=[bm("cvt")], tag="cvk")
            P.op("SC", (lambda oc=oc: nc.scalar.activation(cooked[:, oc, :],
                                                           cvt_b[:], AF.Silu)),
                 reads=[bm("cvt")], writes=[bm("cooked", oc)], tag="cvsl")
            if oc < 16:
                P.op("V", (lambda oc=oc: nc.vector.tensor_mul(
                    cvt_b[:], cooked[:, oc, :], cooked[:, oc, :])),
                     reads=[bm("cooked", oc)], writes=[bm("cvt")], tag="l2sq")
                P.op("PE", lambda: nc.tensor.matmul(
                    B[4][0:1, :], ones_cb[:], cvt_b[:], start=True, stop=True),
                     reads=[bm("cvt"), bm("ones_cb")], writes=[bm("B", 4)],
                     tag="l2mm")
                P.op("SC", lambda: nc.scalar.activation(
                    rrow_f[0:1, :], B[4][0:1, :], AF.Ln, bias=eps6[0:1, :]),
                     reads=[bm("B", 4), bm("eps6")], writes=[bm("rrow")], tag="l2ln")
                P.op("SC", lambda: nc.scalar.activation(
                    rrow_b[0:1, :], rrow_f[0:1, :], AF.Exp, scale=-0.5),
                     reads=[bm("rrow")], writes=[bm("rrowb")], tag="l2ex")
                for c in range(4):
                    P.op("PE", (lambda c=c: nc.tensor.matmul(
                        B[4][:, c + 2:c + 3],
                        rrow_b[0:1, c * 128:(c + 1) * 128], one_b[0:1, :],
                        start=True, stop=True)),
                         reads=[bm("rrowb"), bm("one_b")], writes=[bm("B", 4)],
                         tag="rcm")
                if oc < 8:
                    P.op("V", (lambda oc=oc: nc.vector.tensor_scalar(
                        rqcols[:, oc, :], B[4][:, 2:6], DKSC, None, ALU.mult)),
                         reads=[bm("B", 4)], writes=[bm("rqc", oc)], tag="rqcv")
                else:
                    P.op("V", (lambda oc=oc: nc.vector.tensor_copy(
                        rkcols[:, oc - 8, :], B[4][:, 2:6])),
                         reads=[bm("B", 4)], writes=[bm("rkc", oc - 8)], tag="rkcv")
                    P.op("PE", (lambda: nc.tensor.matmul(
                        B[5][:, :], ones_rb[0:1, :], rrow_b[0:1, :],
                        start=True, stop=True)),
                         reads=[bm("rrowb"), bm("ones_rb")], writes=[bm("B", 5)],
                         tag="krep")
                    P.op("V", (lambda oc=oc: nc.vector.tensor_mul(
                        cooked[:, oc, :], cooked[:, oc, :], B[5][:, :])),
                         reads=[bm("cooked", oc), bm("B", 5)],
                         writes=[bm("cooked", oc)], tag="khat")

        for hc in range(NKC):
            sl = hc % 2
            stream16(s_wg[:, hc, :, :], bm("s_wg", hc), sl, "gld")
            pi = hc % 2
            for kc in range(NKC):
                rds = (wgran16(sl) + [bm("xnT", k2) for k2 in range(NKC)]
                       if kc == 0 else [])
                P.op("PE", (lambda kc=kc, sl=sl, pi=pi: nc.tensor.matmul(
                    B[pi][:, :], wslot16(sl)[:, kc, :], xnT[:, kc, :],
                    start=(kc == 0), stop=(kc == NKC - 1))),
                     reads=rds, reads_nodep=(wgran16(sl) + [bm("xnT", kc)] if kc else []),
                     writes=[bm("B", pi)], tag="gmm")
            P.op("SC", (lambda hc=hc, pi=pi: nc.scalar.activation(
                gateb[:, hc, :], B[pi][:, :], AF.Silu)),
                 reads=[bm("B", pi)], writes=[bm("gateb", hc)], tag="gsl")

        def gu_ops(ic):
            slg = ic % 2
            slu = 2 + ic % 2
            pg, pu = 0, 1
            hs = ic % 2
            yield ("SY", (lambda ic=ic, slg=slg: nc.sync.dma_start(
                wslot16(slg),
                ag_g[ic, :, :].rearrange("p (a b) -> p a b", b=128))),
                [bm("s_fg", ic)], wgran16(slg), "fgld", True)
            yield ("SY", (lambda ic=ic, slu=slu: nc.sync.dma_start(
                wslot16(slu),
                ag_u[ic, :, :].rearrange("p (a b) -> p a b", b=128))),
                [bm("s_fu", ic)], wgran16(slu), "fuld", True)
            for kc in range(NKC):
                rds = (wgran16(slg) + [bm("xnT", k2) for k2 in range(NKC)]
                       if kc == 0 else [])
                nod = [] if kc == 0 else wgran16(slg) + [bm("xnT", kc)]
                yield ("PE", (lambda kc=kc, slg=slg: nc.tensor.matmul(
                    B[0][:, :], wslot16(slg)[:, kc, :], xnT[:, kc, :],
                    start=(kc == 0), stop=(kc == NKC - 1))),
                    rds, [bm("B", 0)], "fgmm", False, nod)
            for kc in range(NKC):
                rds = (wgran16(slu) + [bm("xnT", k2) for k2 in range(NKC)]
                       if kc == 0 else [])
                nod = [] if kc == 0 else wgran16(slu) + [bm("xnT", kc)]
                yield ("PE", (lambda kc=kc, slu=slu: nc.tensor.matmul(
                    B[1][:, :], wslot16(slu)[:, kc, :], xnT[:, kc, :],
                    start=(kc == 0), stop=(kc == NKC - 1))),
                    rds, [bm("B", 1)], "fumm", False, nod)
            yield ("SC", (lambda: nc.scalar.activation(sg_b[:], B[0][:, :],
                                                       AF.Silu)),
                [bm("B", 0)], [bm("sg")], "fsl", False)
            yield ("V", (lambda hs=hs: nc.vector.tensor_copy(
                ub_t[hs][:], B[1][:, :])),
                [bm("B", 1)], [bm("ubt", hs)], "uev", False)
            yield ("V", (lambda hs=hs: nc.vector.tensor_mul(
                hstage[hs][:], sg_b[:], ub_t[hs][:])),
                [bm("sg"), bm("ubt", hs)], [bm("hstage", hs)], "fmul", False)
            yield ("SY", (lambda ic=ic, hs=hs: nc.sync.dma_start(
                s_hact[ic, :, :], hstage[hs][:])),
                [bm("hstage", hs)], [bm("s_hact", ic)], "fhst", True)

        def gu_chain():
            for ic in range(NIC):
                for t in gu_ops(ic):
                    yield t

        gu_gen = gu_chain()
        gu_done = [False]

        def pump_gu_one():
            if gu_done[0]:
                return
            try:
                t = next(gu_gen)
                nod = t[6] if len(t) > 6 else ()
                P.op(t[0], t[1], reads=t[2], writes=t[3], dma=t[5], tag=t[4],
                     reads_nodep=nod)
            except StopIteration:
                gu_done[0] = True

        def pump_gu_rest():
            while not gu_done[0]:
                pump_gu_one()

        for c in range(4):
            cs = slice(c * 128, (c + 1) * 128)
            for q4 in range(4):
                bi = q4 % 2
                for j in range(4):
                    hv = q4 * 4 + j
                    P.op("PE", (lambda hv=hv, j=j, bi=bi, cs=cs:
                                nc.tensor.transpose(BT[bi][:, j * 128:(j + 1) * 128],
                                                    cooked[:, 16 + hv, cs],
                                                    m_idb[:])),
                         reads=[bm("cooked", 16 + hv), bm("m_idb")],
                         writes=[bm("BT", bi)], tag="vtr")
                P.op("V", (lambda q4=q4, bi=bi: nc.vector.tensor_copy(
                    Vt_all[:, q4 * 4:(q4 + 1) * 4, :].rearrange("p a b -> p (a b)"),
                    BT[bi][:, 0:512])),
                     reads=[bm("BT", bi)], writes=[bm("Vt", q4)], tag="vtev")
            for qh in range(HQ):
                P.op("PE", (lambda qh=qh, cs=cs: nc.tensor.transpose(
                    BT[0][:, 0:128], cooked[:, 8 + qh, cs], m_idb[:])),
                     reads=[bm("cooked", 8 + qh), bm("m_idb")], writes=[bm("BT", 0)],
                     tag="ktr")
                P.op("V", (lambda qh=qh: nc.vector.tensor_copy(khatT[:, qh, :],
                                                               BT[0][:, 0:128])),
                     reads=[bm("BT", 0)], writes=[bm("khatT", qh)], tag="ktev")
                P.op("PE", (lambda qh=qh, cs=cs: nc.tensor.matmul(
                    B[2][:, 0:128], cooked[:, 8 + qh, cs], cooked[:, 8 + qh, cs],
                    start=True, stop=True)),
                     reads=[bm("cooked", 8 + qh)], writes=[bm("B", 2)], tag="akk")
                P.op("SC", (lambda qh=qh: nc.scalar.copy(Akk[:, qh, :],
                                                         B[2][:, 0:128])),
                     reads=[bm("B", 2)], writes=[bm("Akk", qh)], tag="akkev")
                P.op("PE", (lambda qh=qh, cs=cs: nc.tensor.matmul(
                    B[2][:, 128:256], cooked[:, 8 + qh, cs], cooked[:, qh, cs],
                    start=True, stop=True)),
                     reads=[bm("cooked", 8 + qh), bm("cooked", qh)], writes=[bm("B", 2)],
                     tag="akq")
                P.op("SC", (lambda qh=qh: nc.scalar.copy(Akq[:, qh, :],
                                                         B[2][:, 128:256])),
                     reads=[bm("B", 2)], writes=[bm("Akq", qh)], tag="akqev")
            for hg in range(4):
                for hh in range(4):
                    hv = hg * 4 + hh
                    P.op("PE", (lambda hv=hv, c=c: nc.tensor.transpose(
                        B[4][0:1, 128:256], Lt[:, c, hv:hv + 1], m_idf[:])),
                         reads=[bm("Lt"), bm("m_idf")], writes=[bm("B", 4)],
                         tag="ltr")
                    P.op("V", lambda: nc.vector.tensor_copy(lrow[0:1, :],
                                                            B[4][0:1, 128:256]),
                         reads=[bm("B", 4)], writes=[bm("lrow")], tag="lrev")
                    P.op("G", (lambda hh=hh: nc.gpsimd.partition_broadcast(
                        Dm[:, hh, :], lrow[0:1, :])),
                         reads=[bm("lrow")], writes=[bm("Dm", hh)], tag="lbc")
                    P.op("V", (lambda hh=hh, hv=hv, c=c: nc.vector.tensor_scalar(
                        Dm[:, hh, :], Dm[:, hh, :], Lt[:, c, hv:hv + 1], 0.0,
                        ALU.subtract, ALU.min)),
                         reads=[bm("Dm", hh), bm("Lt")], writes=[bm("Dm", hh)],
                         tag="dts")
                P.op("SC", (lambda hg=hg: nc.scalar.activation(
                    DbM[:, hg * 4:(hg + 1) * 4, :].rearrange("p a b -> p (a b)"),
                    Dm[:].rearrange("p a b -> p (a b)"), AF.Exp)),
                     reads=[bm("Dm", h) for h in range(4)],
                     writes=[bm("DbM", hg * 4 + h) for h in range(4)], tag="dexp")
                P.op("V", (lambda hg=hg: nc.vector.tensor_mul(
                    DbG[:, hg * 4:(hg + 1) * 4, :].rearrange("p a b -> p (a b)"),
                    DbM[:, hg * 4:(hg + 1) * 4, :].rearrange("p a b -> p (a b)"),
                    m_sneg8[:, 0:512])),
                     reads=[bm("DbM", hg * 4 + h) for h in range(4)] + [bm("m_sneg8")],
                     writes=[bm("DbG", hg * 4 + h) for h in range(4)], tag="dbg")
                P.op("V", (lambda hg=hg: nc.vector.tensor_mul(
                    DbM[:, hg * 4:(hg + 1) * 4, :].rearrange("p a b -> p (a b)"),
                    DbM[:, hg * 4:(hg + 1) * 4, :].rearrange("p a b -> p (a b)"),
                    m_incl8[:, 0:512])),
                     reads=[bm("DbM", hg * 4 + h) for h in range(4)] + [bm("m_incl8")],
                     writes=[bm("DbM", hg * 4 + h) for h in range(4)], tag="dbm")
            for hv in range(HV):
                P.op("V", (lambda hv=hv, c=c: nc.vector.tensor_mul(
                    grq[:, hv:hv + 1], gam[:, c, hv:hv + 1],
                    rqcols[:, hv // 2, c:c + 1])),
                     reads=[bm("gam"), bm("rqc", hv // 2)], writes=[bm("grq", hv)],
                     tag="grq")

            def head_ops(hv, s, c, cs):
                qh = hv // 2
                KQ = B[2] if s == 0 else B[4]
                APb = KQ
                SQ = B[3] if s == 0 else B[5]
                BTs = BT[1] if s == 0 else BT[0]
                kq, sq, bts = ((("B", 2), ("B", 3), ("BT", 1)) if s == 0
                               else (("B", 4), ("B", 5), ("BT", 0)))
                apb = kq
                G2 = Gbuf[s]
                yield ("PE", (lambda: nc.tensor.matmul(
                    KQ[:, 0:128], cooked[:, 8 + qh, cs], Sbf[:, hv, :],
                    start=True, stop=True)),
                    [bm("cooked", 8 + qh), bm("Sbf", hv)], [bm(*kq)], "ks0")
                yield ("PE", (lambda: nc.tensor.matmul(
                    KQ[:, 128:256], cooked[:, qh, cs], Sbf[:, hv, :],
                    start=True, stop=True)),
                    [bm("cooked", qh), bm("Sbf", hv)], [bm(*kq)], "qs0")
                yield ("V", (lambda: nc.vector.scalar_tensor_tensor(
                    G2[0][:, 0:128], DbG[:, hv, :], betaT[:, c, hv:hv + 1],
                    Akk[:, qh, :], ALU.mult, ALU.mult)),
                    [bm("DbG", hv), bm("betaT"), bm("Akk", qh)], [bm("G", s, 0)],
                    "gb2")
                yield ("V", (lambda: nc.vector.tensor_mul(
                    Mp[s][:], DbM[:, hv, :], Akq[:, qh, :])),
                    [bm("DbM", hv), bm("Akq", qh)], [bm("Mp", s)], "mb2")
                yield ("V", (lambda: nc.vector.scalar_tensor_tensor(
                    Rt[s][:], KQ[:, 0:128], gamneg[:, c, hv:hv + 1],
                    Vt_all[:, hv, :], ALU.mult, ALU.add)),
                    [bm(*kq), bm("gamneg"), bm("Vt", hv // 4)], [bm("Rt", s)],
                    "rt")
                yield ("PE", (lambda: nc.tensor.transpose(
                    BTs[:, 0:128], G2[0][:, 0:128], m_idb[:])),
                    [bm("G", s, 0), bm("m_idb")], [bm(*bts)], "g1tr")
                yield ("V", (lambda: nc.vector.tensor_copy(
                    G2[0][:, 128:256], BTs[:, 0:128])),
                    [bm(*bts)], [bm("G", s, 0)], "g1ev")
                for j in range(7):
                    gi = j % 2
                    xi = j % 2
                    xap = Rt[s][:, :] if j == 0 else xch[s][:, (j - 1) % 2, :]
                    xbuf = bm("Rt", s) if j == 0 else bm("xch", s, (j - 1) % 2)
                    if j < 6:
                        yield ("PE", (lambda gi=gi: nc.tensor.matmul(
                            SQ[:, 0:128], G2[gi][:, 128:256], G2[gi][:, 0:128],
                            start=True, stop=True)),
                            [bm("G", s, gi)], [bm(*sq)], "ssq1")
                        yield ("PE", (lambda gi=gi: nc.tensor.matmul(
                            SQ[:, 128:256], G2[gi][:, 0:128], G2[gi][:, 128:256],
                            start=True, stop=True)),
                            [bm("G", s, gi)], [bm(*sq)], "ssq2")
                    yield ("PE", (lambda gi=gi, x=xap: nc.tensor.matmul(
                        APb[:, 0:128], G2[gi][:, 0:128], x,
                        start=True, stop=True)),
                        [bm("G", s, gi), xbuf], [bm(*apb)], "sapg")
                    if j < 6:
                        yield ("SC", (lambda gi=gi: nc.scalar.copy(
                            G2[1 - gi][:, 0:256], SQ[:, 0:256])),
                            [bm(*sq)], [bm("G", s, 1 - gi)], "ssqev")
                    yield ("V", (lambda xi=xi, x=xap: nc.vector.tensor_add(
                        xch[s][:, xi, :], APb[:, 0:128], x)),
                        [bm(*apb), xbuf], [bm("xch", s, xi)], "sapev")
                yield ("SC", (lambda: nc.scalar.activation(
                    Ub[s][:], xch[s][:, 0, :], AF.Copy,
                    scale=betaT[:, c, hv:hv + 1])),
                    [bm("xch", s, 0), bm("betaT")], [bm("Ub", s)], "ub")
                yield ("PE", (lambda: nc.tensor.matmul(
                    APb[:, 0:128], Mp[s][:], Ub[s][:], start=True, stop=True)),
                    [bm("Mp", s), bm("Ub", s)], [bm(*apb)], "oin")
                yield ("SC", (lambda: nc.scalar.activation(
                    ea_f[s][:], KQ[:, 128:256], AF.Copy,
                    scale=grq[:, hv:hv + 1])),
                    [bm(*kq), bm("grq", hv)], [bm("ea", s)], "oea")
                yield ("V", (lambda: nc.vector.scalar_tensor_tensor(
                    O_f[s][:], APb[:, 0:128], rqcols[:, qh, c:c + 1], ea_f[s][:],
                    ALU.mult, ALU.add)),
                    [bm(*apb), bm("rqc", qh), bm("ea", s)], [bm("Of", s)], "osum")
                yield ("SC", (lambda: nc.scalar.activation(
                    osq[s][:], O_f[s][:], AF.Square, accum_out=ossq[s][:])),
                    [bm("Of", s)], [bm("osq", s), bm("ossq", s)], "orsq")
                yield ("SC", (lambda: nc.scalar.activation(
                    orf[s][:], ossq[s][:], AF.Ln, scale=1.0 / 128,
                    bias=eps6[:])),
                    [bm("ossq", s), bm("eps6")], [bm("orf", s)], "orln")
                yield ("SC", (lambda: nc.scalar.activation(
                    orf[s][:], orf[s][:], AF.Exp, scale=-0.5)),
                    [bm("orf", s)], [bm("orf", s)], "orex")
                yield ("SC", (lambda: nc.scalar.activation(
                    On_b[s][:], O_f[s][:], AF.Copy, scale=orf[s][:])),
                    [bm("Of", s), bm("orf", s)], [bm("Onb", s)], "onb")
                yield ("PE", (lambda: nc.tensor.transpose(
                    BTs[:, 128:256], On_b[s][:], m_idb[:])),
                    [bm("Onb", s), bm("m_idb")], [bm(*bts)], "ctr")
                yield ("V", (lambda: nc.vector.scalar_tensor_tensor(
                    gateb[:, hv, cs], gateb[:, hv, cs], rmsw[:],
                    BTs[:, 128:256], ALU.mult, ALU.mult)),
                    [bm("gateb", hv), bm("rmsw"), bm(*bts)],
                    [bm("gateb", hv)], "gmul")
                yield ("SC", (lambda: nc.scalar.activation(
                    Kdec[s][:], khatT[:, qh, :], AF.Copy,
                    scale=eKD[:, c, hv:hv + 1])),
                    [bm("khatT", qh), bm("eKD")], [bm("Kdec", s)], "kdec")
                yield ("PE", (lambda: nc.tensor.matmul(
                    APb[:, 128:256], Kdec[s][:], Ub[s][:],
                    start=True, stop=True)),
                    [bm("Kdec", s), bm("Ub", s)], [bm(*apb)], "snmm")
                yield ("V", (lambda: nc.vector.scalar_tensor_tensor(
                    S32[:, hv, :], S32[:, hv, :], grep[:, c, hv:hv + 1],
                    APb[:, 128:256], ALU.mult, ALU.add)),
                    [bm("S32", hv), bm("grep"), bm(*apb)], [bm("S32", hv)],
                    "sup")
                yield ("V", (lambda: nc.vector.tensor_copy(
                    Sbf[:, hv, :], S32[:, hv, :])),
                    [bm("S32", hv)], [bm("Sbf", hv)], "sbf")

            state = {"next": 0, "lanes": [None, None]}

            def refill(s):
                if state["next"] < HV:
                    state["lanes"][s] = head_ops(state["next"], s, c, cs)
                    state["next"] += 1
                else:
                    state["lanes"][s] = None

            def step(s):
                g = state["lanes"][s]
                if g is None:
                    return False
                try:
                    (e, em, rd, wr, tg) = next(g)
                    P.op(e, em, reads=rd, writes=wr, tag=tg)
                    return True
                except StopIteration:
                    refill(s)
                    return step(s)

            refill(0)
            refill(1)
            for _ in range(38):
                step(0)
            while state["lanes"][0] is not None or state["lanes"][1] is not None:
                step(0)
                step(1)
                pump_gu_one()

        pump_gu_rest()

        for hc in range(NKC):
            sl = hc % 2
            stream16(s_wo[:, hc, :, :], bm("s_wo", hc), sl, "old")
            pi = hc % 2
            for kc in range(NKC):
                rds = (wgran16(sl) + [bm("gateb", k2) for k2 in range(NKC)]
                       if kc == 0 else [])
                P.op("PE", (lambda kc=kc, sl=sl, pi=pi: nc.tensor.matmul(
                    B[pi][:, :], wslot16(sl)[:, kc, :], gateb[:, kc, :],
                    start=(kc == 0), stop=(kc == NKC - 1))),
                     reads=rds, reads_nodep=(wgran16(sl) + [bm("gateb", kc)] if kc else []),
                     writes=[bm("B", pi)], tag="omm")
            P.op("V", (lambda hc=hc, pi=pi: nc.vector.tensor_copy(
                accb[:, hc, :], B[pi][:, :])),
                 reads=[bm("B", pi)], writes=[bm("accb", hc)], tag="oev")

        for (h0, h1) in ((0, 6), (6, 12), (12, 16)):
            nb = h1 - h0
            for oc in range(NIC):
                P.op("SY", (lambda oc=oc, r=oc % 4: nc.sync.dma_start(
                    hring[r][:], s_hact[oc, :, :])),
                     reads=[bm("s_hact", oc)], writes=[bm("hring", oc % 4)],
                     dma=True, tag="fdhl")
                wsl = oc % 2
                P.op("SY", (lambda oc=oc, wsl=wsl, h0=h0, h1=h1, nb=nb:
                            nc.sync.dma_start(
                    wpool[:, wsl * 6144:wsl * 6144 + nb * 128].rearrange(
                        "p (a b) -> p a b", b=128),
                    ag_d[h0:h1, :, oc * 128:(oc + 1) * 128]
                        .transpose([1, 0, 2]))),
                     reads=[bm("s_fd", hh) for hh in range(h0, h1)],
                     writes=wgran43(wsl), dma=True, tag="fdwl")
                for k in range(nb):
                    P.op("PE", (lambda oc=oc, wsl=wsl, k=k: nc.tensor.matmul(
                        B[k][:, :],
                        wpool[:, wsl * 6144 + k * 128:wsl * 6144 + (k + 1) * 128],
                        hring[oc % 4][:], start=(oc == 0),
                        stop=(oc == NIC - 1))),
                         reads=wgran43(wsl) + [bm("hring", oc % 4)],
                         writes=[bm("B", k)], tag="fdmm")
            for k in range(nb):
                P.op("V", (lambda hc=h0 + k, k=k: nc.vector.tensor_add(
                    accb[:, hc, :], accb[:, hc, :], B[k][:, :])),
                     reads=[bm("accb", h0 + k), bm("B", k)],
                     writes=[bm("accb", h0 + k)], tag="fdev")

        for tc in range(4):
            r0 = t0 + tc * 128
            P.op("SY", (lambda r0=r0: nc.sync.dma_start(xout[:], x_d[r0:r0 + 128, :])),
                 writes=[bm("xout")], dma=True, tag="xold")
            for half in range(2):
                k0 = half * 8
                for hc in range(k0, k0 + 8):
                    P.op("PE", (lambda hc=hc, k0=k0, bi=half, tc=tc:
                                nc.tensor.transpose(
                                    BT[bi][:, (hc - k0) * 128:(hc - k0 + 1) * 128],
                                    accb[:, hc, tc * 128:(tc + 1) * 128], m_idb[:])),
                         reads=[bm("accb", hc), bm("m_idb")],
                         writes=[bm("BT", half)], tag="otr")
                P.op("V", (lambda k0=k0, bi=half: nc.vector.tensor_add(
                    xout[:, k0 * 128:(k0 + 8) * 128],
                    xout[:, k0 * 128:(k0 + 8) * 128], BT[bi][:, 0:1024])),
                     reads=[bm("BT", half), bm("xout")], writes=[bm("xout")],
                     tag="oadd")
            P.op("SY", (lambda r0=r0: nc.sync.dma_start(out_d[r0:r0 + 128, :],
                                                        xout[:])),
                 reads=[bm("xout")], writes=[bm("outd", r0)], dma=True, tag="ost")

    P.finalize(nc, sems, lane_sems)
    st.close()
    return nc


def _consts():
    ii = np.arange(128)
    return {
        "cid_bf": np.eye(128, dtype=np.float32).astype(ml_dtypes.bfloat16),
        "cid_f": np.eye(128, dtype=np.float32),
        "cm_incl": (ii[:, None] <= ii[None, :]).astype(np.float32)
                    .astype(ml_dtypes.bfloat16),
        "cm_sneg": (-(ii[:, None] < ii[None, :]).astype(np.float32))
                    .astype(ml_dtypes.bfloat16),
        "cm_incl8": np.tile((ii[:, None] <= ii[None, :]).astype(np.float32),
                            (1, 8)).astype(ml_dtypes.bfloat16),
        "cm_sneg8": np.tile(-(ii[:, None] < ii[None, :]).astype(np.float32),
                            (1, 8)).astype(ml_dtypes.bfloat16),
    }


def make_in_maps(L0_attn_x, L0_attn_w_qkv, L0_attn_w_gate, L0_attn_w_beta,
                 L0_attn_w_alpha, L0_attn_log_A, L0_attn_dt_bias,
                 L0_attn_conv_w, L0_attn_rms_w, L0_attn_w_out, L0_ffn_lut,
                 L0_ffn_gate_idx, L0_ffn_up_idx, L0_ffn_down_idx):
    x = np.ascontiguousarray(np.asarray(L0_attn_x, dtype=np.float32))
    shared = {
        "w_qkv": np.ascontiguousarray(np.asarray(L0_attn_w_qkv, np.float32)),
        "w_gate": np.ascontiguousarray(np.asarray(L0_attn_w_gate, np.float32)),
        "w_out": np.ascontiguousarray(np.asarray(L0_attn_w_out, np.float32)),
        "w_ba": np.ascontiguousarray(np.concatenate(
            [np.asarray(L0_attn_w_beta, np.float32),
             np.asarray(L0_attn_w_alpha, np.float32)], axis=0)),
        "conv_w": np.ascontiguousarray(
            np.asarray(L0_attn_conv_w, np.float32).reshape(4096, 4)),
        "log_A": np.ascontiguousarray(
            np.asarray(L0_attn_log_A, np.float32).reshape(16, 1)),
        "dt_bias": np.ascontiguousarray(
            np.asarray(L0_attn_dt_bias, np.float32).reshape(16, 1)),
        "rms_w": np.ascontiguousarray(
            np.asarray(L0_attn_rms_w, np.float32).reshape(128, 1)),
    }
    shared.update(_consts())
    gidx = np.asarray(L0_ffn_gate_idx, np.int32)
    uidx = np.asarray(L0_ffn_up_idx, np.int32)
    didx = np.asarray(L0_ffn_down_idx, np.int32)
    gpad = np.concatenate([gidx, np.tile(gidx[-128:], (5, 1))], axis=0)
    upad = np.concatenate([uidx, np.tile(uidx[-128:], (5, 1))], axis=0)
    maps = []
    for b in range(x.shape[0]):
        maps.append(dict(
            shared, x=np.ascontiguousarray(x[b]),
            my_g=np.ascontiguousarray(gpad[b * 768:(b + 1) * 768]),
            my_u=np.ascontiguousarray(upad[b * 768:(b + 1) * 768]),
            my_d=np.ascontiguousarray(didx[b * 256:(b + 1) * 256]),
        ))
    return maps


def kernel(**inputs):
    in_maps = make_in_maps(**inputs)
    lut_vals = [float(v) for v in np.asarray(inputs["L0_ffn_lut"],
                                             np.float32).reshape(16)]
    nc = build_program(NT_FULL, lut_vals=lut_vals)
    mybir.codegen_inst_isa_subclasses(nc)
    res = run_bass_kernel_spmd(nc, in_maps, core_ids=list(range(8)))
    return np.stack([np.asarray(r["out"], dtype=np.float32)
                     for r in res.results], axis=0)
